# version 10
# baseline (speedup 1.0000x reference)
import numpy as np
import ml_dtypes
from contextlib import ExitStack

import concourse.bass as bass
import concourse.mybir as mybir
from concourse.bass_utils import run_bass_kernel_spmd

F32 = mybir.dt.float32
BF16 = mybir.dt.bfloat16
AF = mybir.ActivationFunctionType
ALU = mybir.AluOpType
AX = mybir.AxisListType
BT = ml_dtypes.bfloat16

D = 1024
KC = 8
T = 2048
HALO = 256
TH = T + HALO
TD = 512
NT = T // TD
NQT = T // 128
WIN = 384
NH = 16
HD = 64
FF = 16384
EPS = 1e-5


class Ctr:
    def __init__(self, sem):
        self.sem = sem
        self.n = 0


class Res:

    def __init__(self):
        self.fa = None

    def acq(self, q):
        if self.fa is not None:
            q.wait(self.fa[0], self.fa[1])

    def rel(self, ctr):
        self.fa = (ctr.sem, ctr.n)


class Q:

    def __init__(self, name):
        self.name = name
        self.items = []

    def wait(self, sem, n):
        if n > 0:
            self.items.append(("w", sem, n))

    def op(self, fn, inc=None, k=1):
        self.items.append(("o", fn, None if inc is None else inc.sem, k))
        if inc is not None:
            inc.n += k

    def dma(self, out_ap, in_ap, inc):
        self.op(lambda e, o=out_ap, i=in_ap: e.dma_start(out=o, in_=i), inc, 16)

    def emit(self, eng):
        for it in self.items:
            if it[0] == "w":
                eng.wait_ge(it[1], it[2])
            else:
                _, fn, sem, k = it
                inst = fn(eng)
                if sem is not None:
                    inst.then_inc(sem, k)


def build_program(stop=None):
    nc = bass.Bass("TRN2", target_bir_lowering=False)

    x_h = nc.dram_tensor("x", [D, TH], F32, kind="ExternalInput").ap()
    wqt_h = nc.dram_tensor("wqt", [D, D], F32, kind="ExternalInput").ap()
    wkd_h = nc.dram_tensor("wkd", [D, 128], F32, kind="ExternalInput").ap()
    wvt_h = nc.dram_tensor("wvt", [D, HD], F32, kind="ExternalInput").ap()
    wft_h = nc.dram_tensor("wft", [D, D], F32, kind="ExternalInput").ap()
    wct_h = nc.dram_tensor("wct", [FF, D], F32, kind="ExternalInput").ap()
    bfr_h = nc.dram_tensor("bfr", [128, KC], F32, kind="ExternalInput").ap()
    mka_h = nc.dram_tensor("mka", [128, 1024], F32, kind="ExternalInput").ap()
    upb_h = nc.dram_tensor("upb", [128, 4096], F32, kind="ExternalInput").ap()
    masks_h = nc.dram_tensor("masks", [128, 3 * WIN], F32, kind="ExternalInput").ap()
    ident_h = nc.dram_tensor("ident", [128, 128], F32, kind="ExternalInput").ap()
    out_h = nc.dram_tensor("out", [D, T], F32, kind="ExternalOutput").ap()
    dbg_h = nc.dram_tensor("dbg", [128, 8192], F32, kind="ExternalOutput").ap()

    ctx = ExitStack()
    sb = lambda n, s, d: ctx.enter_context(nc.sbuf_tensor(n, s, d))

    x32 = sb("x32", [128, KC * T], F32)
    kT2 = sb("kT2", [128, TH], BF16)
    v_tm = sb("v_tm", [128, (TH // 128) * HD], BF16)
    mka_sb = sb("mka_sb", [128, 1024], BF16)
    upb_sb = sb("upb_sb", [128, 4096], BF16)
    bf_sb = sb("bf_sb", [128, KC], F32)
    id_bf = sb("id_bf", [128, 128], BF16)
    ones_bf = sb("ones_bf", [128, 1], BF16)
    ones_f = sb("ones_f", [1, 128], F32)
    wc_buf = sb("wc_buf", [128, 2 * 4 * 1024], BF16)
    lnrow = sb("lnrow", [1, 4 * TD], F32)
    rb_buf = sb("rb_buf", [128, 2 * TD], BF16)
    eps_sb = sb("eps_sb", [1, 1], F32)
    masks_sb = sb("masks_sb", [128, 3 * WIN], BF16)
    sq_buf = sb("sq_buf", [128, 2 * TD], BF16)

    arA = bass.AutoArena(nc)
    xt_buf = arA.sbuf_tensor("xt_buf", [128, 2 * KC * TD], BF16)
    wq_sb = arA.sbuf_tensor("wq_sb", [128, KC * 1024], BF16)
    wkd_sb = arA.sbuf_tensor("wkd_sb", [128, KC * 128], BF16)
    wvt_sb = arA.sbuf_tensor("wvt_sb", [128, KC * HD], BF16)
    v_fm = arA.sbuf_tensor("v_fm", [64, TH], BF16)

    arQ = bass.AutoArena(nc)
    qT = arQ.sbuf_tensor("qT", [128, KC * T], BF16)

    arB = bass.AutoArena(nc)
    attn = arB.sbuf_tensor("attn", [128, KC * T], BF16)
    s_sb = arB.sbuf_tensor("s_sb", [128, 4 * WIN], F32)
    p_bf = arB.sbuf_tensor("p_bf", [128, 4 * WIN], BF16)
    pt_sb = arB.sbuf_tensor("pt_sb", [128, 4 * WIN], BF16)
    nmax = arB.sbuf_tensor("nmax", [128, 4], F32)
    rsum = arB.sbuf_tensor("rsum", [128, 4], F32)
    rcp = arB.sbuf_tensor("rcp", [128, 4], F32)
    wf_buf = arB.sbuf_tensor("wf_buf", [128, 2 * 1024], BF16)

    arC = bass.AutoArena(nc)
    x2b = arC.sbuf_tensor("x2b", [128, KC * TD], BF16)
    y_q = arC.sbuf_tensor("y_q", [128, 8 * TD], BF16)
    h_q = arC.sbuf_tensor("h_q", [128, 32 * TD], BF16)
    acc = arC.sbuf_tensor("acc", [128, KC * TD], F32)
    o_buf = arC.sbuf_tensor("o_buf", [128, 2 * TD], F32)

    arA.allocate(side="left")
    arQ.allocate(side="right")
    arA.free()
    arB.allocate(side="left")
    arB.free()
    arQ.free()
    arC.allocate(side="left")
    arC.free()

    semS = ctx.enter_context(nc.semaphore("semS"))
    semG = ctx.enter_context(nc.semaphore("semG"))
    semT = ctx.enter_context(nc.semaphore("semT"))
    semV = ctx.enter_context(nc.semaphore("semV"))
    semA = ctx.enter_context(nc.semaphore("semA"))
    semO = ctx.enter_context(nc.semaphore("semO"))
    cS, cG = Ctr(semS), Ctr(semG)
    cT, cV, cA, cO = Ctr(semT), Ctr(semV), Ctr(semA), Ctr(semO)
    cX = [Ctr(ctx.enter_context(nc.semaphore(f"semX{i}"))) for i in range(2)]
    cW = [Ctr(ctx.enter_context(nc.semaphore(f"semW{i}"))) for i in range(2)]
    cC = [Ctr(ctx.enter_context(nc.semaphore(f"semC{i}"))) for i in range(2)]
    cO2 = [Ctr(ctx.enter_context(nc.semaphore(f"semO{i}"))) for i in range(2)]
    cG2 = Ctr(ctx.enter_context(nc.semaphore("semG2")))
    cG3 = Ctr(ctx.enter_context(nc.semaphore("semG3")))

    qE, qV, qA, qG, qS = Q("tensor"), Q("vector"), Q("scalar"), Q("gpsimd"), Q("sync")

    psum_ctx = ExitStack()

    qS.dma(x32[:].rearrange("p (k t) -> p k t", k=KC),
           x_h[:, HALO:TH].rearrange("(k p) t -> p k t", p=128), cS)
    qS.dma(bf_sb[:], bfr_h, cS)

    qG.dma(wkd_sb[:].rearrange("p (k m) -> p k m", k=KC),
           wkd_h.rearrange("(k p) m -> p k m", p=128), cG)
    qG.dma(wvt_sb[:].rearrange("p (k m) -> p k m", k=KC),
           wvt_h.rearrange("(k p) m -> p k m", p=128), cG)
    qG.op(lambda e: e.memset(ones_bf[:], 1.0))
    qG.op(lambda e: e.memset(eps_sb[:], EPS))
    qG.op(lambda e: e.memset(ones_f[:], 1.0), cG, 1)
    g_kv = cG.n

    kv_tiles = [(i * TD, TD) for i in range(4)] + [(4 * TD, HALO)]
    q_tiles = [(HALO + i * TD, TD) for i in range(NT)]
    all_tiles = kv_tiles + q_tiles
    xt_ready = [None] * len(all_tiles)
    xt_res = [Res(), Res()]

    def load_xt(bi):
        c0, w = all_tiles[bi]
        half = (bi % 2) * KC * TD
        xt_res[bi % 2].acq(qG)
        qG.dma(xt_buf[:, half:half + KC * w].rearrange(
                   "p (k t) -> p k t", k=KC),
               x_h[:, c0:c0 + w].rearrange("(k p) t -> p k t", p=128), cX[bi % 2])
        xt_ready[bi] = (cX[bi % 2].sem, cX[bi % 2].n)

    load_xt(0)
    load_xt(1)
    qG.dma(id_bf[:], ident_h, cG2)
    qG.dma(masks_sb[:], masks_h, cG2)
    g_w0 = cG2.n
    qG.dma(wq_sb[:].rearrange("p (k m) -> p k m", k=KC),
           wqt_h.rearrange("(k p) m -> p k m", p=128), cG3)
    qG.dma(mka_sb[:], mka_h, cG3)
    qG.dma(upb_sb[:], upb_h, cG3)
    g_w1 = cG3.n

    KV0s = [psum_ctx.enter_context(nc.psum_tensor(f"kv0{i}", [128, 512], F32))
            for i in range(2)]
    KV1s = [psum_ctx.enter_context(nc.psum_tensor(f"kv1{i}", [128, 512], F32))
            for i in range(2)]
    QP = [psum_ctx.enter_context(nc.psum_tensor(f"qp{i}", [128, 512], F32))
          for i in range(2)]
    VT = [psum_ctx.enter_context(nc.psum_tensor(f"vt{i}", [128, 1024], BF16))
          for i in range(2)]
    rKV0s = [Res(), Res()]
    rKV1s = [Res(), Res()]
    rQP = [Res(), Res()]
    rVT = [Res(), Res()]

    qE.wait(semG, g_kv)

    for bi, (c0, w) in enumerate(kv_tiles):
        half = (bi % 2) * KC * TD
        pb = bi % 2
        KV0, KV1 = KV0s[pb], KV1s[pb]
        qE.wait(*xt_ready[bi])
        rKV0s[pb].acq(qE)
        for k in range(KC):
            qE.op(lambda e, k=k, h=half, w=w, KV0=KV0: e.matmul(
                KV0[:, 0:w], wkd_sb[:, 128 * k:128 * k + 128],
                xt_buf[:, h + k * w:h + k * w + w],
                start=(k == 0), stop=(k == KC - 1)),
                cT if k == KC - 1 else None, 1)
        t_k = cT.n
        rKV1s[pb].acq(qE)
        for k in range(KC):
            qE.op(lambda e, k=k, h=half, w=w, KV1=KV1: e.matmul(
                KV1[0:64, 0:w], wvt_sb[:, 64 * k:64 * k + 64],
                xt_buf[:, h + k * w:h + k * w + w],
                start=(k == 0), stop=(k == KC - 1)),
                cT if k == KC - 1 else None, 1)
        t_v = cT.n
        xt_res[bi % 2].rel(cT)
        if bi + 2 < len(all_tiles):
            load_xt(bi + 2)
        qV.wait(semT, t_k)
        qV.op(lambda e, c0=c0, w=w, KV0=KV0: e.tensor_copy(
            kT2[:, c0:c0 + w], KV0[:, 0:w]), cV, 1)
        rKV0s[pb].rel(cV)
        qV.wait(semT, t_v)
        qV.op(lambda e, c0=c0, w=w, KV1=KV1: e.tensor_copy(
            v_fm[:, c0:c0 + w], KV1[0:64, 0:w]), cV, 1)
        rKV1s[pb].rel(cV)
    v_fm_ready = cV.n

    qE.wait(cG2.sem, g_w0)
    qE.wait(semV, v_fm_ready)
    for st in range(TH // 128):
        b = st % 2
        rVT[b].acq(qE)
        qE.op(lambda e, st=st, b=b: e.transpose(
            VT[b][:, 0:64], v_fm[:, 128 * st:128 * st + 128], id_bf[0:64, 0:64]),
            cT, 1)
        qV.wait(semT, cT.n)
        qV.op(lambda e, st=st, b=b: e.tensor_copy(
            v_tm[:, 64 * st:64 * st + 64], VT[b][:, 0:64]), cV, 1)
        rVT[b].rel(cV)
    v_tm_ready = cV.n
    t_vtr = cT.n

    qE.wait(cG3.sem, g_w1)
    for ti in range(NT):
        bi = 5 + ti
        half = (bi % 2) * KC * TD
        qE.wait(*xt_ready[bi])
        for m in range(KC):
            pb = m % 2
            rQP[pb].acq(qE)
            for k in range(KC):
                qE.op(lambda e, k=k, m=m, h=half, pb=pb: e.matmul(
                    QP[pb][:, 0:TD],
                    wq_sb[:, 1024 * k + 128 * m:1024 * k + 128 * m + 128],
                    xt_buf[:, h + k * TD:h + k * TD + TD],
                    start=(k == 0), stop=(k == KC - 1)),
                    cT if k == KC - 1 else None, 1)
            qA.wait(semT, cT.n)
            qA.op(lambda e, m=m, ti=ti, pb=pb: e.activation(
                out=qT[:, T * m + TD * ti:T * m + TD * ti + TD],
                in_=QP[pb][:, 0:TD], func=AF.Copy, bias=0.0, scale=0.125), cA, 1)
            rQP[pb].rel(cA)
        xt_res[bi % 2].rel(cT)
        if bi + 2 < len(all_tiles):
            load_xt(bi + 2)
    q_done_a = cA.n
    psum_ctx.close()

    def _finish_dbg(srcs):
        off = 0
        for ap, wdt in srcs:
            qG.dma(dbg_h[:, off:off + wdt], ap, cO)
            off += wdt
        qG.wait(semO, cO.n)

    if stop == "p1":
        qG.wait(semV, v_tm_ready)
        _finish_dbg([(kT2[:, 0:2304], 2304), (v_tm[:, 0:1152], 1152)])
        return _emit_and_close(nc, ctx, qE, qV, qA, qG, qS)
    if stop == "p2":
        qG.wait(semA, q_done_a)
        _finish_dbg([(qT[:, 0:8192], 8192)])
        return _emit_and_close(nc, ctx, qE, qV, qA, qG, qS)

    psum_ctx = ExitStack()
    SP = [psum_ctx.enter_context(nc.psum_tensor(f"sp{i}", [128, 512], F32))
          for i in range(4)]
    TP = [psum_ctx.enter_context(nc.psum_tensor(f"tp{i}", [128, 1024], BF16))
          for i in range(2)]
    OP = [psum_ctx.enter_context(nc.psum_tensor(f"op{i}", [128, 512], F32))
          for i in range(2)]
    rSP = [Res() for _ in range(4)]
    rTP = [Res() for _ in range(2)]
    rOP = [Res() for _ in range(2)]
    r_ssb = [Res() for _ in range(4)]
    r_pbf = [Res() for _ in range(4)]
    r_pts = [Res() for _ in range(4)]
    r_nm = [Res() for _ in range(4)]
    r_rcp = [Res() for _ in range(4)]

    qE.wait(semA, q_done_a)
    qE.wait(semV, cV.n)
    qV.wait(cG2.sem, g_w0)

    s_ready = {}
    norm_ready = {}
    tr_ready = {}
    pt_ready = {}
    pv_ready = {}
    exp_after = {}

    def task_tuple(t):
        return t // NH, (t % NH) // 2, t % 2

    p3_qts = NQT
    p3_beta = True
    p3_level = 4
    if stop is not None and stop.startswith("p3") and ":" in stop:
        kind, narg = stop.split(":")
        if kind == "p3l":
            p3_qts = 1
            p3_beta = False
            p3_level = int(narg)
        else:
            p3_qts = int(narg)
            p3_beta = (kind != "p3s")
            if not p3_beta:
                p3_level = 3
    if stop == "p3z":
        p3_qts = 0
    if stop is not None and stop.startswith("p3y"):
        ny = int(stop.split(":")[1]) if ":" in stop else 4
        p3_qts = 0
        for i in range(ny):
            h2 = i % 2
            sbk = i % 4
            rSP[sbk].acq(qE)
            qE.op(lambda e, i=i, h2=h2, sbk=sbk: e.matmul(
                SP[sbk][:, 0:WIN],
                qT[64 * h2:64 * h2 + 64, 128 * (i % 16):128 * (i % 16) + 128],
                kT2[64 * h2:64 * h2 + 64, 0:WIN],
                start=True, stop=True, tile_position=(64 * h2, 0)), cT, 1)
            qV.wait(semT, cT.n)
            qV.op(lambda e, i=i, sbk=sbk: e.tensor_copy(
                s_sb[:, WIN * (i % 2):WIN * (i % 2) + WIN],
                SP[sbk][:, 0:WIN]), cV, 1)
            rSP[sbk].rel(cV)

    red_done = {}
    exp_cnt = {}
    recip_done = {}
    norm_cnt = {}
    total_tasks = p3_qts * NH
    beta_on = p3_beta and p3_level >= 4

    def emit_S(t):
        qt, p, h2 = task_tuple(t)
        sbk = t % 4
        if t >= 4:
            qE.wait(semV, red_done[t - 4])
        qE.op(lambda e, p=p, h2=h2, qt=qt, sbk=sbk: e.matmul(
            SP[sbk][:, 0:WIN],
            qT[64 * h2:64 * h2 + 64, T * p + 128 * qt:T * p + 128 * qt + 128],
            kT2[64 * h2:64 * h2 + 64, 128 * qt:128 * qt + WIN],
            start=True, stop=True, tile_position=(64 * h2, 0)), cT, 1)
        s_ready[t] = cT.n

    def emit_front(t):
        qt = t // NH
        mi = 1 if qt == 0 else (2 if qt == 1 else 0)
        b = t % 4
        qV.wait(semT, s_ready[t])
        if t >= 4:
            qV.wait(semA, exp_cnt.get(t - 4, 0))
        qV.op(lambda e, t=t, b=b, mi=mi: e.tensor_tensor(
            out=s_sb[:, WIN * b:WIN * b + WIN],
            in0=SP[t % 4][:, 0:WIN],
            in1=masks_sb[:, WIN * mi:WIN * mi + WIN], op=ALU.add))
        qV.op(lambda e, t=t, b=b: e.tensor_reduce(
            out=nmax[:, b:b + 1], in_=s_sb[:, WIN * b:WIN * b + WIN],
            axis=AX.X, op=ALU.max, negate=True), cV, 1)
        red_done[t] = cV.n
        qA.wait(semV, red_done[t])
        if t >= 4:
            qA.wait(semV, norm_cnt.get(t - 4, 0))
        qA.op(lambda e, b=b: e.activation(
            out=s_sb[:, WIN * b:WIN * b + WIN],
            in_=s_sb[:, WIN * b:WIN * b + WIN], func=AF.Exp,
            bias=nmax[:, b:b + 1], scale=1.0, accum_out=rsum[:, b:b + 1]))
        qA.op(lambda e, b=b: e.activation(
            out=rcp[:, b:b + 1], in_=rsum[:, b:b + 1], func=AF.Ln))
        qA.op(lambda e, b=b: e.activation(
            out=rcp[:, b:b + 1], in_=rcp[:, b:b + 1], func=AF.Exp, scale=-1.0),
            cA, 1)
        exp_cnt[t] = cA.n


    def emit_norm(t):
        b = t % 4
        qV.wait(semA, exp_cnt[t])
        if t - 4 in tr_ready:
            qV.wait(semT, tr_ready[t - 4])
        qV.op(lambda e, b=b: e.tensor_scalar(
            out=p_bf[:, WIN * b:WIN * b + WIN],
            in0=s_sb[:, WIN * b:WIN * b + WIN],
            scalar1=rcp[:, b:b + 1], scalar2=None, op0=ALU.mult), cV, 1)
        norm_cnt[t] = cV.n

    def emit_tr(t):
        b = t % 4
        tb = t % 2
        qE.wait(semV, norm_cnt[t])
        rTP[tb].acq(qE)
        for j in range(3):
            qE.op(lambda e, tb=tb, b=b, j=j: e.transpose(
                TP[tb][:, 128 * j:128 * j + 128],
                p_bf[:, WIN * b + 128 * j:WIN * b + 128 * j + 128],
                id_bf[:]), cT if j == 2 else None, 1)
        tr_ready[t] = cT.n
        if t % 2 == 0:
            qA.wait(semT, tr_ready[t])
            r_pts[b].acq(qA)
            qA.op(lambda e, tb=tb, b=b: e.activation(
                out=pt_sb[:, WIN * b:WIN * b + WIN],
                in_=TP[tb][:, 0:WIN], func=AF.Copy, bias=0.0, scale=1.0), cA, 1)
            pt_ready[t] = (semA, cA.n)
            rTP[tb].rel(cA)
        else:
            qV.wait(semT, tr_ready[t])
            r_pts[b].acq(qV)
            qV.op(lambda e, tb=tb, b=b: e.tensor_copy(
                pt_sb[:, WIN * b:WIN * b + WIN], TP[tb][:, 0:WIN]), cV, 1)
            pt_ready[t] = (semV, cV.n)
            rTP[tb].rel(cV)

    def emit_pv(t):
        qt_, p, h2 = task_tuple(t)
        b = t % 4
        ob = (t // 2) % 2
        if h2 == 0:
            rOP[ob].acq(qE)
        qE.wait(*pt_ready[t])
        for j in range(3):
            qE.op(lambda e, qt_=qt_, h2=h2, b=b, j=j, ob=ob: e.matmul(
                OP[ob][64 * h2:64 * h2 + 64, 0:128],
                v_tm[:, 64 * (qt_ + j):64 * (qt_ + j) + 64],
                pt_sb[:, WIN * b + 128 * j:WIN * b + 128 * j + 128],
                start=(j == 0), stop=(j == 2), tile_position=(0, 64 * h2)),
                cT if j == 2 else None, 1)
        pv_ready[t] = cT.n
        r_pts[b].rel(cT)
        if h2 == 1:
            qV.wait(semT, pv_ready[t])
            qV.op(lambda e, qt_=qt_, p=p, ob=ob: e.tensor_copy(
                attn[:, T * p + 128 * qt_:T * p + 128 * qt_ + 128],
                OP[ob][:, 0:128]), cV, 1)
            rOP[ob].rel(cV)

    for t in range(min(4, total_tasks)):
        emit_S(t)
    for t in range(total_tasks + 3):
        if t < total_tasks:
            emit_front(t)
            if t + 4 < total_tasks:
                emit_S(t + 4)
        if 0 <= t - 2 < total_tasks and p3_level >= 3:
            emit_norm(t - 2)
            if beta_on:
                emit_tr(t - 2)
        if 0 <= t - 3 < total_tasks and beta_on:
            emit_pv(t - 3)

    attn_done_v = cV.n
    psum_ctx.close()

    if stop is not None and stop.startswith("p3"):
        qG.wait(semV, attn_done_v)
        qG.wait(semA, cA.n)
        if stop in ("p3z",) or stop.startswith("p3y"):
            qG.wait(semT, cT.n)
            _finish_dbg([(kT2[:, 0:2304], 2304)])
        elif stop.startswith("p3s") or stop.startswith("p3l"):
            _finish_dbg([(s_sb[:, 0:2 * WIN], 2 * WIN),
                         (p_bf[:, 0:2 * WIN], 2 * WIN),
                         (nmax[:, 0:2], 2), (rsum[:, 0:2], 2), (rcp[:, 0:2], 2)])
        elif stop.startswith("p3q"):
            qG.wait(semT, cT.n)
            _finish_dbg([(pt_sb[:, 0:2 * WIN], 2 * WIN),
                         (attn[:, 0:7424], 7424)])
        else:
            _finish_dbg([(kT2[:, 0:2304], 2304), (v_tm[:, 0:1152], 1152),
                         (attn[:, 0:4736], 4736)])
        return _emit_and_close(nc, ctx, qE, qV, qA, qG, qS)

    r_rb = [Res(), Res()]
    r_sq = [Res(), Res()]
    r_lnrow = Res()
    r_obuf = [Res(), Res()]

    def emit_ln(ti, out_mode, P0, P1, P2, P3, rP0, rP1, rP2, rP3):
        col = TD * ti
        qA.wait(semV, cV.n)
        rb_ready = {}
        sq_done = {}
        statsmm_t = {}
        sqmm_t = {}
        for c in range(KC):
            b = c % 2
            if c >= 2:
                qA.wait(semT, statsmm_t[c - 2])
                qA.wait(semV, sq_done[c - 2])
            qA.op(lambda e, c=c, b=b, col=col: e.activation(
                out=rb_buf[:, TD * b:TD * b + TD],
                in_=x32[:, T * c + col:T * c + col + TD],
                func=AF.Copy, bias=0.0, scale=1.0), cA, 1)
            rb_ready[c] = cA.n
            qV.wait(semA, rb_ready[c])
            if c >= 2:
                qV.wait(semT, sqmm_t[c - 2])
            qV.op(lambda e, b=b: e.tensor_tensor(
                out=sq_buf[:, TD * b:TD * b + TD],
                in0=rb_buf[:, TD * b:TD * b + TD],
                in1=rb_buf[:, TD * b:TD * b + TD], op=ALU.mult), cV, 1)
            sq_done[c] = cV.n
            qE.wait(semA, rb_ready[c])
            if c == 0:
                rP0.acq(qE)
            qE.op(lambda e, c=c, b=b: e.matmul(
                P0[0:1, 0:TD], ones_bf[:], rb_buf[:, TD * b:TD * b + TD],
                start=(c == 0), stop=(c == KC - 1)), cT, 1)
            statsmm_t[c] = cT.n
            qE.wait(semV, sq_done[c])
            if c == 0:
                rP1.acq(qE)
            qE.op(lambda e, c=c, b=b: e.matmul(
                P1[0:1, 0:TD], ones_bf[:], sq_buf[:, TD * b:TD * b + TD],
                start=(c == 0), stop=(c == KC - 1)), cT, 1)
            sqmm_t[c] = cT.n
        t_stats = cT.n
        qV.wait(semT, t_stats)
        r_lnrow.acq(qV)
        qV.op(lambda e: e.tensor_scalar(
            out=lnrow[:, 0:TD], in0=P0[0:1, 0:TD], scalar1=1.0 / D, scalar2=None,
            op0=ALU.mult))
        qV.op(lambda e: e.tensor_scalar(
            out=lnrow[:, TD:2 * TD], in0=P1[0:1, 0:TD], scalar1=1.0 / D,
            scalar2=None, op0=ALU.mult), cV, 1)
        rP0.rel(cV)
        rP1.rel(cV)
        qV.op(lambda e: e.tensor_tensor(
            out=lnrow[:, 2 * TD:3 * TD], in0=lnrow[:, 0:TD], in1=lnrow[:, 0:TD],
            op=ALU.mult))
        qV.op(lambda e: e.tensor_tensor(
            out=lnrow[:, TD:2 * TD], in0=lnrow[:, TD:2 * TD],
            in1=lnrow[:, 2 * TD:3 * TD], op=ALU.subtract), cV, 1)
        v_var = cV.n
        qA.wait(semV, v_var)
        qA.op(lambda e: e.activation(
            out=lnrow[:, TD:2 * TD], in_=lnrow[:, TD:2 * TD], func=AF.Ln,
            bias=eps_sb[:], scale=1.0))
        qA.op(lambda e: e.activation(
            out=lnrow[:, TD:2 * TD], in_=lnrow[:, TD:2 * TD], func=AF.Exp,
            scale=-0.5), cA, 1)
        a_rstd = cA.n
        qV.wait(semA, a_rstd)
        qV.op(lambda e: e.tensor_tensor(
            out=lnrow[:, 3 * TD:4 * TD], in0=lnrow[:, 0:TD],
            in1=lnrow[:, TD:2 * TD], op=ALU.mult), cV, 1)
        v_mr = cV.n
        qE.wait(semV, v_mr)
        rP2.acq(qE)
        qE.op(lambda e: e.matmul(
            P2[:, 0:TD], ones_f[:], lnrow[:, TD:2 * TD], start=True, stop=True),
            cT, 1)
        rP3.acq(qE)
        qE.op(lambda e: e.matmul(
            P3[:, 0:TD], ones_f[:], lnrow[:, 3 * TD:4 * TD], start=True, stop=True),
            cT, 1)
        t_bc = cT.n
        r_lnrow.rel(cT)
        qV.wait(semT, t_bc)
        for c in range(KC):
            if out_mode == "x32":
                qV.op(lambda e, c=c, col=col: e.tensor_tensor(
                    out=x32[:, T * c + col:T * c + col + TD],
                    in0=x32[:, T * c + col:T * c + col + TD],
                    in1=P2[:, 0:TD], op=ALU.mult))
                qV.op(lambda e, c=c, col=col: e.tensor_tensor(
                    out=x32[:, T * c + col:T * c + col + TD],
                    in0=x32[:, T * c + col:T * c + col + TD],
                    in1=P3[:, 0:TD], op=ALU.subtract), cV, 1)
            else:
                b = c % 2
                r_obuf[b].acq(qV)
                qV.op(lambda e, c=c, col=col, b=b: e.tensor_tensor(
                    out=o_buf[:, TD * b:TD * b + TD],
                    in0=x32[:, T * c + col:T * c + col + TD],
                    in1=P2[:, 0:TD], op=ALU.mult))
                qV.op(lambda e, b=b: e.tensor_tensor(
                    out=o_buf[:, TD * b:TD * b + TD],
                    in0=o_buf[:, TD * b:TD * b + TD],
                    in1=P3[:, 0:TD], op=ALU.subtract), cV, 1)
                qS.wait(semV, cV.n)
                qS.dma(out_h[128 * c:128 * c + 128, col:col + TD],
                       o_buf[:, TD * b:TD * b + TD], cO2[b])
                r_obuf[b].rel(cO2[b])
        rP2.rel(cV)
        rP3.rel(cV)
        return cV.n

    psum_ctx = ExitStack()
    FP = [psum_ctx.enter_context(nc.psum_tensor(f"fp{i}", [128, 512], F32))
          for i in range(8)]
    rFP = [Res() for _ in range(8)]
    r_wf = [Res(), Res()]
    wf_ready = {}

    def load_wf(ti, k):
        b = k % 2
        r_wf[b].acq(qG)
        qG.dma(wf_buf[:, 1024 * b:1024 * b + 1024],
               wft_h[128 * k:128 * k + 128, :], cW[b])
        wf_ready[(ti, k)] = (cW[b].sem, cW[b].n)

    qE.wait(semV, attn_done_v)
    qE.wait(semA, cA.n)
    qV.wait(semS, cS.n)

    qG.wait(semT, t_vtr)
    load_wf(0, 0)
    load_wf(0, 1)
    ln1_done = {}
    for ti in range(NT):
        for k in range(KC):
            b = k % 2
            qE.wait(*wf_ready[(ti, k)])
            for m in range(KC):
                if k == 0:
                    rFP[m].acq(qE)
                qE.op(lambda e, k=k, m=m, b=b, ti=ti: e.matmul(
                    FP[m][:, 0:TD],
                    wf_buf[:, 1024 * b + 128 * m:1024 * b + 128 * m + 128],
                    attn[:, T * k + TD * ti:T * k + TD * ti + TD],
                    start=(k == 0), stop=(k == KC - 1)),
                    cT if m == KC - 1 else None, 1)
            r_wf[b].rel(cT)
            if k + 2 < KC:
                load_wf(ti, k + 2)
            elif ti + 1 < NT:
                load_wf(ti + 1, k + 2 - KC)
        t_wf = cT.n
        qV.wait(semT, t_wf)
        for m in range(KC):
            qV.op(lambda e, m=m, ti=ti: e.scalar_tensor_tensor(
                out=x32[:, T * m + TD * ti:T * m + TD * ti + TD],
                in0=FP[m][:, 0:TD], scalar=bf_sb[:, m:m + 1],
                in1=x32[:, T * m + TD * ti:T * m + TD * ti + TD],
                op0=ALU.add, op1=ALU.add), cV, 1)
            rFP[m].rel(cV)
        if stop == "p4r":
            ln1_done[ti] = cV.n
        else:
            ln1_done[ti] = emit_ln(ti, "x32", FP[0], FP[1], FP[2], FP[3],
                                   rFP[0], rFP[1], rFP[2], rFP[3])
    psum_ctx.close()

    if stop in ("p4", "p4r"):
        qG.wait(semV, cV.n)
        _finish_dbg([(x32[:, 0:8192], 8192)])
        return _emit_and_close(nc, ctx, qE, qV, qA, qG, qS)

    psum_ctx = ExitStack()
    CP = [psum_ctx.enter_context(nc.psum_tensor(f"cp{i}", [128, 512], F32))
          for i in range(8)]
    rCP = [Res() for _ in range(8)]
    r_wc = [Res(), Res()]
    r_x2b = Res()
    r_yq = [Res() for _ in range(8)]
    r_hq = Res()

    qE.wait(semV, cV.n)
    qE.wait(semA, cA.n)

    wc_seq = [(ti, qq, h, half2) for ti in range(NT) for qq in range(4)
              for h in range(4) for half2 in range(2)]
    wc_ready = {}

    def load_wc(i):
        ti, qq, h, half2 = wc_seq[i]
        b = i % 2
        r_wc[b].acq(qG)
        row0 = 128 * (32 * h + 8 * qq + 4 * half2)
        qG.dma(wc_buf[:, 4096 * b:4096 * b + 4096].rearrange(
                   "p (c m) -> p c m", c=4),
               wct_h[row0:row0 + 512, :].rearrange("(c p) m -> p c m", p=128),
               cC[b])
        wc_ready[i] = (cC[b].sem, cC[b].n, b)

    load_wc(0)
    load_wc(1)
    wc_i = 0

    gelu_ready = {}
    pending_ln = [None]
    for ti in range(NT):
        col = TD * ti
        qA.wait(semV, ln1_done[ti])
        r_x2b.acq(qA)
        r_x2b.acq(qV)
        for c in range(KC):
            if c % 2 == 0:
                qA.op(lambda e, c=c, col=col: e.activation(
                    out=x2b[:, TD * c:TD * c + TD],
                    in_=x32[:, T * c + col:T * c + col + TD],
                    func=AF.Copy, bias=0.0, scale=1.0), cA, 1)
            else:
                qV.op(lambda e, c=c, col=col: e.tensor_copy(
                    x2b[:, TD * c:TD * c + TD],
                    x32[:, T * c + col:T * c + col + TD]), cV, 1)
        x2b_a, x2b_v = cA.n, cV.n

        for qq in range(4):
            qE.wait(semA, x2b_a)
            qE.wait(semV, x2b_v)
            emitted_ln = False
            y_ready = {}
            for rpp in range(8):
                rp = 8 * qq + rpp
                g4 = rp % 4
                bank = rpp % 2
                rCP[bank].acq(qE)
                qE.op(lambda e, rp=rp, g4=g4, bank=bank: e.matmul(
                    CP[bank][:, 0:TD],
                    mka_sb[32 * g4:32 * g4 + 32,
                           128 * (rp // 4):128 * (rp // 4) + 128],
                    x2b[32 * g4:32 * g4 + 32, TD * (rp // 4):TD * (rp // 4) + TD],
                    start=True, stop=True, tile_position=(32 * g4, 0)), cT, 1)
                qV.wait(semT, cT.n)
                r_yq[rpp].acq(qV)
                qV.op(lambda e, rpp=rpp, bank=bank: e.tensor_copy(
                    y_q[:, TD * rpp:TD * rpp + TD], CP[bank][:, 0:TD]), cV, 1)
                y_ready[rpp] = cV.n
                rCP[bank].rel(cV)
            if qq == 0 and pending_ln[0] is not None:
                emit_ln(pending_ln[0], "store", CP[6], CP[7], CP[6], CP[7],
                        rCP[6], rCP[7], rCP[6], rCP[7])
                pending_ln[0] = None
            first_gelu = True
            for rpp in range(8):
                rp = 8 * qq + rpp
                qE.wait(semV, y_ready[rpp])
                for h in range(4):
                    bank = 2 + h
                    rCP[bank].acq(qE)
                    qE.op(lambda e, rp=rp, rpp=rpp, h=h, bank=bank: e.matmul(
                        CP[bank][:, 0:TD],
                        upb_sb[32 * h:32 * h + 32, 128 * rp:128 * rp + 128],
                        y_q[32 * h:32 * h + 32, TD * rpp:TD * rpp + TD],
                        start=True, stop=True, tile_position=(32 * h, 0)), cT, 1)
                    t_z = cT.n
                    qA.wait(semT, t_z)
                    if first_gelu:
                        r_hq.acq(qA)
                        first_gelu = False
                    lidx = 8 * h + rpp
                    qA.op(lambda e, lidx=lidx, bank=bank: e.activation(
                        out=h_q[:, TD * lidx:TD * lidx + TD],
                        in_=CP[bank][:, 0:TD], func=AF.Gelu), cA, 1)
                    gelu_ready[lidx] = cA.n
                    rCP[bank].rel(cA)
                r_yq[rpp].rel(cT)
            first = True
            for h in range(4):
                for half2 in range(2):
                    gsem, gn, wb = wc_ready[wc_i]
                    qE.wait(gsem, gn)
                    for cc in range(4):
                        rpg = 4 * half2 + cc
                        lidx = 8 * h + rpg
                        qE.wait(semA, gelu_ready[lidx])
                        if first:
                            for m in range(KC):
                                rCP[m].acq(qE)
                        for m in range(KC):
                            qE.op(lambda e, wb=wb, cc=cc, m=m, lidx=lidx,
                                  st=first, sp=(h == 3 and rpg == 7): e.matmul(
                                CP[m][:, 0:TD],
                                wc_buf[:, 4096 * wb + 1024 * cc + 128 * m:
                                       4096 * wb + 1024 * cc + 128 * m + 128],
                                h_q[:, TD * lidx:TD * lidx + TD],
                                start=st, stop=sp),
                                cT if (m == KC - 1 and cc == 3) else None, 1)
                        first = False
                    r_wc[wb].rel(cT)
                    if wc_i + 2 < len(wc_seq):
                        load_wc(wc_i + 2)
                    wc_i += 1
            t_cproj = cT.n
            r_hq.rel(cT)
            qV.wait(semT, t_cproj)
            for m in range(KC):
                if qq == 0:
                    qV.op(lambda e, m=m: e.tensor_copy(
                        acc[:, TD * m:TD * m + TD], CP[m][:, 0:TD]), cV, 1)
                else:
                    qV.op(lambda e, m=m: e.tensor_tensor(
                        out=acc[:, TD * m:TD * m + TD],
                        in0=acc[:, TD * m:TD * m + TD],
                        in1=CP[m][:, 0:TD], op=ALU.add), cV, 1)
                rCP[m].rel(cV)
        for c in range(KC):
            qV.op(lambda e, c=c, col=col: e.tensor_tensor(
                out=x32[:, T * c + col:T * c + col + TD],
                in0=x32[:, T * c + col:T * c + col + TD],
                in1=acc[:, TD * c:TD * c + TD], op=ALU.add), cV, 1)
        r_x2b.fa = (semT, cT.n)
        pending_ln[0] = ti
    emit_ln(pending_ln[0], "store", CP[6], CP[7], CP[6], CP[7],
            rCP[6], rCP[7], rCP[6], rCP[7])

    qS.wait(cO2[0].sem, cO2[0].n)
    qS.wait(cO2[1].sem, cO2[1].n)
    return _emit_and_close(nc, ctx, qE, qV, qA, qG, qS)


def _emit_and_close(nc, ctx, qE, qV, qA, qG, qS):
    with nc.Block() as block:
        @block.tensor
        def _(e):
            qE.emit(e)

        @block.vector
        def _(e):
            qV.emit(e)

        @block.scalar
        def _(e):
            qA.emit(e)

        @block.gpsimd
        def _(e):
            qG.emit(e)

        @block.sync
        def _(e):
            qS.emit(e)

    ctx.close()
    return nc



def _host_prep(inputs):
    _ = inputs.get("L0_attn_segment_pos")
    x = np.asarray(inputs["L0_attn_x"], dtype=np.float32)
    Wq = np.asarray(inputs["L0_attn_Wq"], dtype=np.float32)
    Wk = np.asarray(inputs["L0_attn_Wk"], dtype=np.float32)
    Wv = np.asarray(inputs["L0_attn_Wv"], dtype=np.float32)
    Wf = np.asarray(inputs["L0_attn_Wf"], dtype=np.float32)
    bfv = np.asarray(inputs["L0_attn_bf"], dtype=np.float32)
    mk = np.asarray(inputs["L0_ffn_rail_maker"], dtype=np.float32)
    up = np.asarray(inputs["L0_ffn_rail_up_proj"], dtype=np.float32)
    Wc = np.asarray(inputs["L0_ffn_w_cproj"], dtype=np.float32)

    shared = {}
    shared["wqt"] = np.ascontiguousarray(Wq.T)
    shared["wkd"] = np.ascontiguousarray(np.concatenate([Wk.T, Wk.T], axis=1))
    shared["wvt"] = np.ascontiguousarray(Wv.T)
    shared["wft"] = np.ascontiguousarray(Wf.T)
    shared["wct"] = np.ascontiguousarray(Wc.T)
    shared["bfr"] = np.ascontiguousarray(bfv.reshape(KC, 128).T)
    shared["ident"] = np.eye(128, dtype=np.float32)

    mka = np.zeros((128, 1024), dtype=np.float32)
    h_, r_, i_, j_ = np.indices((4, 64, 16, 16))
    rp, rr = r_ // 2, r_ % 2
    p = 32 * (rp % 4) + 16 * rr + i_
    cc = 128 * (rp // 4) + 32 * h_ + 16 * rr + j_
    mka[p.ravel(), cc.ravel()] = mk.ravel()
    shared["mka"] = mka

    upb = np.zeros((128, 4096), dtype=np.float32)
    h_, r_, k_, j_ = np.indices((4, 64, 64, 16))
    rp, rr = r_ // 2, r_ % 2
    p = 32 * h_ + 16 * rr + j_
    cc = 128 * rp + 64 * rr + k_
    upb[p.ravel(), cc.ravel()] = up.ravel()
    shared["upb"] = upb

    i_g, j_g = np.indices((128, WIN))
    band = (j_g >= i_g) & (j_g <= i_g + 256)
    neg = np.float32(-1e30)
    m0 = np.where(band, np.float32(0.0), neg).astype(np.float32)
    m1s = np.where(band & (j_g >= 256), np.float32(0.0), neg)
    m2s = np.where(band & (j_g >= 128), np.float32(0.0), neg)
    masks_plain = np.ascontiguousarray(np.concatenate([m0, m0, m0], axis=1))
    masks_start = np.ascontiguousarray(
        np.concatenate([m0, m1s, m2s], axis=1).astype(np.float32))

    in_maps = []
    for core in range(8):
        b = core // 4
        r0 = (core % 4) * T
        xs = np.zeros((TH, D), dtype=np.float32)
        lo = max(0, r0 - HALO)
        xs[HALO - (r0 - lo):] = x[b, lo:r0 + T]
        d = dict(shared)
        d["x"] = np.ascontiguousarray(xs.T)
        d["masks"] = masks_start if r0 == 0 else masks_plain
        in_maps.append(d)
    return in_maps


def kernel(**inputs):
    in_maps = _host_prep(inputs)
    nc = build_program()
    res = run_bass_kernel_spmd(nc, in_maps, core_ids=list(range(8)))
    out = np.empty((2, 8192, D), dtype=np.float32)
    for core in range(8):
        b = core // 4
        r0 = (core % 4) * T
        out[b, r0:r0 + T, :] = res.results[core]["out"].T
    return out


if __name__ == "__main__":
    nc = build_program()
    print("built ok")


# version 11 (final)
# speedup vs baseline: 1.0630x; 1.0630x over previous
import numpy as np
import ml_dtypes
from contextlib import ExitStack

import concourse.bass as bass
import concourse.mybir as mybir
from concourse.bass_utils import run_bass_kernel_spmd

F32 = mybir.dt.float32
BF16 = mybir.dt.bfloat16
AF = mybir.ActivationFunctionType
ALU = mybir.AluOpType
AX = mybir.AxisListType
BT = ml_dtypes.bfloat16

D = 1024
KC = 8
T = 2048
HALO = 256
TH = T + HALO
TD = 512
NT = T // TD
NQT = T // 128
WIN = 384
NH = 16
HD = 64
FF = 16384
EPS = 1e-5


class Ctr:
    def __init__(self, sem):
        self.sem = sem
        self.n = 0


class Res:

    def __init__(self):
        self.fa = None

    def acq(self, q):
        if self.fa is not None:
            q.wait(self.fa[0], self.fa[1])

    def rel(self, ctr):
        self.fa = (ctr.sem, ctr.n)


class Q:

    def __init__(self, name):
        self.name = name
        self.items = []

    def wait(self, sem, n):
        if n > 0:
            self.items.append(("w", sem, n))

    def op(self, fn, inc=None, k=1):
        self.items.append(("o", fn, None if inc is None else inc.sem, k))
        if inc is not None:
            inc.n += k

    def dma(self, out_ap, in_ap, inc):
        self.op(lambda e, o=out_ap, i=in_ap: e.dma_start(out=o, in_=i), inc, 16)

    def emit(self, eng):
        for it in self.items:
            if it[0] == "w":
                eng.wait_ge(it[1], it[2])
            else:
                _, fn, sem, k = it
                inst = fn(eng)
                if sem is not None:
                    inst.then_inc(sem, k)


def build_program(stop=None):
    nc = bass.Bass("TRN2", target_bir_lowering=False)

    x_h = nc.dram_tensor("x", [D, TH], F32, kind="ExternalInput").ap()
    wqt_h = nc.dram_tensor("wqt", [D, D], F32, kind="ExternalInput").ap()
    wkd_h = nc.dram_tensor("wkd", [D, 128], F32, kind="ExternalInput").ap()
    wvt_h = nc.dram_tensor("wvt", [D, HD], F32, kind="ExternalInput").ap()
    wft_h = nc.dram_tensor("wft", [D, D], F32, kind="ExternalInput").ap()
    wct_h = nc.dram_tensor("wct", [FF, D], F32, kind="ExternalInput").ap()
    bfr_h = nc.dram_tensor("bfr", [128, KC], F32, kind="ExternalInput").ap()
    mka_h = nc.dram_tensor("mka", [128, 1024], F32, kind="ExternalInput").ap()
    upb_h = nc.dram_tensor("upb", [128, 4096], F32, kind="ExternalInput").ap()
    masks_h = nc.dram_tensor("masks", [128, 3 * WIN], F32, kind="ExternalInput").ap()
    ident_h = nc.dram_tensor("ident", [128, 128], F32, kind="ExternalInput").ap()
    out_h = nc.dram_tensor("out", [D, T], F32, kind="ExternalOutput").ap()
    wctb_h = nc.dram_tensor("wctb", [FF, D], BF16, kind="Internal").ap()
    dbg_h = nc.dram_tensor("dbg", [128, 8192], F32, kind="ExternalOutput").ap()

    ctx = ExitStack()
    sb = lambda n, s, d: ctx.enter_context(nc.sbuf_tensor(n, s, d))

    x32 = sb("x32", [128, KC * T], F32)
    kT2 = sb("kT2", [128, TH], BF16)
    v_tm = sb("v_tm", [128, (TH // 128) * HD], BF16)
    mka_sb = sb("mka_sb", [128, 1024], BF16)
    upb_sb = sb("upb_sb", [128, 4096], BF16)
    bf_sb = sb("bf_sb", [128, KC], F32)
    id_bf = sb("id_bf", [128, 128], BF16)
    ones_bf = sb("ones_bf", [128, 1], BF16)
    ones_f = sb("ones_f", [1, 128], F32)
    wc_buf = sb("wc_buf", [128, 2 * 4 * 1024], BF16)
    lnrow = sb("lnrow", [1, 4 * TD], F32)
    rb_buf = sb("rb_buf", [128, 2 * TD], BF16)
    eps_sb = sb("eps_sb", [1, 1], F32)
    masks_sb = sb("masks_sb", [128, 3 * WIN], BF16)
    sq_buf = sb("sq_buf", [128, 2 * TD], BF16)

    arA = bass.AutoArena(nc)
    xt_buf = arA.sbuf_tensor("xt_buf", [128, 2 * KC * TD], BF16)
    wq_sb = arA.sbuf_tensor("wq_sb", [128, KC * 1024], BF16)
    wkd_sb = arA.sbuf_tensor("wkd_sb", [128, KC * 128], BF16)
    wvt_sb = arA.sbuf_tensor("wvt_sb", [128, KC * HD], BF16)
    v_fm = arA.sbuf_tensor("v_fm", [64, TH], BF16)

    arQ = bass.AutoArena(nc)
    qT = arQ.sbuf_tensor("qT", [128, KC * T], BF16)

    arB = bass.AutoArena(nc)
    attn = arB.sbuf_tensor("attn", [128, KC * T], BF16)
    s_sb = arB.sbuf_tensor("s_sb", [128, 4 * WIN], F32)
    p_bf = arB.sbuf_tensor("p_bf", [128, 4 * WIN], BF16)
    pt_sb = arB.sbuf_tensor("pt_sb", [128, 4 * WIN], BF16)
    nmax = arB.sbuf_tensor("nmax", [128, 4], F32)
    rsum = arB.sbuf_tensor("rsum", [128, 4], F32)
    rcp = arB.sbuf_tensor("rcp", [128, 4], F32)
    wf_buf = arB.sbuf_tensor("wf_buf", [128, 2 * 1024], BF16)

    arC = bass.AutoArena(nc)
    x2b = arC.sbuf_tensor("x2b", [128, KC * TD], BF16)
    y_q = arC.sbuf_tensor("y_q", [128, 8 * TD], BF16)
    h_q = arC.sbuf_tensor("h_q", [128, 32 * TD], BF16)
    acc = arC.sbuf_tensor("acc", [128, KC * TD], F32)
    o_buf = arC.sbuf_tensor("o_buf", [128, 2 * TD], F32)

    arA.allocate(side="left")
    arQ.allocate(side="right")
    arA.free()
    arB.allocate(side="left")
    arB.free()
    arQ.free()
    arC.allocate(side="left")
    arC.free()

    semS = ctx.enter_context(nc.semaphore("semS"))
    semG = ctx.enter_context(nc.semaphore("semG"))
    semT = ctx.enter_context(nc.semaphore("semT"))
    semV = ctx.enter_context(nc.semaphore("semV"))
    semA = ctx.enter_context(nc.semaphore("semA"))
    semO = ctx.enter_context(nc.semaphore("semO"))
    cS, cG = Ctr(semS), Ctr(semG)
    cT, cV, cA, cO = Ctr(semT), Ctr(semV), Ctr(semA), Ctr(semO)
    cX = [Ctr(ctx.enter_context(nc.semaphore(f"semX{i}"))) for i in range(2)]
    cW = [Ctr(ctx.enter_context(nc.semaphore(f"semW{i}"))) for i in range(2)]
    cC = [Ctr(ctx.enter_context(nc.semaphore(f"semC{i}"))) for i in range(2)]
    cO2 = [Ctr(ctx.enter_context(nc.semaphore(f"semO{i}"))) for i in range(2)]
    cGc = Ctr(ctx.enter_context(nc.semaphore("semGc")))

    qE, qV, qA, qG, qS = Q("tensor"), Q("vector"), Q("scalar"), Q("gpsimd"), Q("sync")

    psum_ctx = ExitStack()

    qS.dma(x32[:].rearrange("p (k t) -> p k t", k=KC),
           x_h[:, HALO:TH].rearrange("(k p) t -> p k t", p=128), cS)
    qS.dma(bf_sb[:], bfr_h, cS)

    qG.dma(id_bf[:], ident_h, cG)
    g_id = cG.n
    qG.dma(masks_sb[:], masks_h, cG)
    g_masks = cG.n
    qG.dma(wkd_sb[:].rearrange("p (k m) -> p k m", k=KC),
           wkd_h.rearrange("(k p) m -> p k m", p=128), cG)
    qG.dma(wvt_sb[:].rearrange("p (k m) -> p k m", k=KC),
           wvt_h.rearrange("(k p) m -> p k m", p=128), cG)
    qG.dma(mka_sb[:], mka_h, cG)
    qG.dma(upb_sb[:], upb_h, cG)
    qG.dma(wq_sb[:].rearrange("p (k m) -> p k m", k=KC),
           wqt_h.rearrange("(k p) m -> p k m", p=128), cG)
    qG.op(lambda e: e.memset(ones_bf[:], 1.0))
    qG.op(lambda e: e.memset(eps_sb[:], EPS))
    qG.op(lambda e: e.memset(ones_f[:], 1.0), cG, 1)
    g_w0 = cG.n

    kv_tiles = [(i * TD, TD) for i in range(4)] + [(4 * TD, HALO)]
    q_tiles = [(HALO + i * TD, TD) for i in range(NT)]
    all_tiles = kv_tiles + q_tiles
    xt_ready = [None] * len(all_tiles)
    xt_res = [Res(), Res()]

    def load_xt(bi):
        c0, w = all_tiles[bi]
        half = (bi % 2) * KC * TD
        xt_res[bi % 2].acq(qG)
        qG.dma(xt_buf[:, half:half + KC * w].rearrange(
                   "p (k t) -> p k t", k=KC),
               x_h[:, c0:c0 + w].rearrange("(k p) t -> p k t", p=128), cX[bi % 2])
        xt_ready[bi] = (cX[bi % 2].sem, cX[bi % 2].n)

    load_xt(0)
    load_xt(1)

    KV0s = [psum_ctx.enter_context(nc.psum_tensor(f"kv0{i}", [128, 512], F32))
            for i in range(2)]
    KV1s = [psum_ctx.enter_context(nc.psum_tensor(f"kv1{i}", [128, 512], F32))
            for i in range(2)]
    QP = [psum_ctx.enter_context(nc.psum_tensor(f"qp{i}", [128, 512], F32))
          for i in range(2)]
    VT = [psum_ctx.enter_context(nc.psum_tensor(f"vt{i}", [128, 1024], BF16))
          for i in range(2)]
    rKV0s = [Res(), Res()]
    rKV1s = [Res(), Res()]
    rQP = [Res(), Res()]
    rVT = [Res(), Res()]

    qE.wait(semG, g_w0)

    for bi, (c0, w) in enumerate(kv_tiles):
        half = (bi % 2) * KC * TD
        pb = bi % 2
        KV0, KV1 = KV0s[pb], KV1s[pb]
        qE.wait(*xt_ready[bi])
        rKV0s[pb].acq(qE)
        for k in range(KC):
            qE.op(lambda e, k=k, h=half, w=w, KV0=KV0: e.matmul(
                KV0[:, 0:w], wkd_sb[:, 128 * k:128 * k + 128],
                xt_buf[:, h + k * w:h + k * w + w],
                start=(k == 0), stop=(k == KC - 1)),
                cT if k == KC - 1 else None, 1)
        t_k = cT.n
        rKV1s[pb].acq(qE)
        for k in range(KC):
            qE.op(lambda e, k=k, h=half, w=w, KV1=KV1: e.matmul(
                KV1[0:64, 0:w], wvt_sb[:, 64 * k:64 * k + 64],
                xt_buf[:, h + k * w:h + k * w + w],
                start=(k == 0), stop=(k == KC - 1)),
                cT if k == KC - 1 else None, 1)
        t_v = cT.n
        xt_res[bi % 2].rel(cT)
        if bi + 2 < len(all_tiles):
            load_xt(bi + 2)
        qV.wait(semT, t_k)
        qV.op(lambda e, c0=c0, w=w, KV0=KV0: e.tensor_copy(
            kT2[:, c0:c0 + w], KV0[:, 0:w]), cV, 1)
        rKV0s[pb].rel(cV)
        qV.wait(semT, t_v)
        qV.op(lambda e, c0=c0, w=w, KV1=KV1: e.tensor_copy(
            v_fm[:, c0:c0 + w], KV1[0:64, 0:w]), cV, 1)
        rKV1s[pb].rel(cV)
    v_fm_ready = cV.n

    qE.wait(semV, v_fm_ready)
    for st in range(TH // 128):
        b = st % 2
        rVT[b].acq(qE)
        qE.op(lambda e, st=st, b=b: e.transpose(
            VT[b][:, 0:64], v_fm[:, 128 * st:128 * st + 128], id_bf[0:64, 0:64]),
            cT, 1)
        qV.wait(semT, cT.n)
        qV.op(lambda e, st=st, b=b: e.tensor_copy(
            v_tm[:, 64 * st:64 * st + 64], VT[b][:, 0:64]), cV, 1)
        rVT[b].rel(cV)
    v_tm_ready = cV.n
    t_vtr = cT.n

    for ti in range(NT):
        bi = 5 + ti
        half = (bi % 2) * KC * TD
        qE.wait(*xt_ready[bi])
        for m in range(KC):
            pb = m % 2
            rQP[pb].acq(qE)
            for k in range(KC):
                qE.op(lambda e, k=k, m=m, h=half, pb=pb: e.matmul(
                    QP[pb][:, 0:TD],
                    wq_sb[:, 1024 * k + 128 * m:1024 * k + 128 * m + 128],
                    xt_buf[:, h + k * TD:h + k * TD + TD],
                    start=(k == 0), stop=(k == KC - 1)),
                    cT if k == KC - 1 else None, 1)
            qA.wait(semT, cT.n)
            qA.op(lambda e, m=m, ti=ti, pb=pb: e.activation(
                out=qT[:, T * m + TD * ti:T * m + TD * ti + TD],
                in_=QP[pb][:, 0:TD], func=AF.Copy, bias=0.0, scale=0.125), cA, 1)
            rQP[pb].rel(cA)
        xt_res[bi % 2].rel(cT)
        if bi + 2 < len(all_tiles):
            load_xt(bi + 2)
    q_done_a = cA.n
    psum_ctx.close()

    def _finish_dbg(srcs):
        off = 0
        for ap, wdt in srcs:
            qG.dma(dbg_h[:, off:off + wdt], ap, cO)
            off += wdt
        qG.wait(semO, cO.n)

    if stop == "p1":
        qG.wait(semV, v_tm_ready)
        _finish_dbg([(kT2[:, 0:2304], 2304), (v_tm[:, 0:1152], 1152)])
        return _emit_and_close(nc, ctx, qE, qV, qA, qG, qS)
    if stop == "p2":
        qG.wait(semA, q_done_a)
        _finish_dbg([(qT[:, 0:8192], 8192)])
        return _emit_and_close(nc, ctx, qE, qV, qA, qG, qS)

    psum_ctx = ExitStack()
    SP = [psum_ctx.enter_context(nc.psum_tensor(f"sp{i}", [128, 512], F32))
          for i in range(4)]
    TP = [psum_ctx.enter_context(nc.psum_tensor(f"tp{i}", [128, 1024], BF16))
          for i in range(2)]
    OP = [psum_ctx.enter_context(nc.psum_tensor(f"op{i}", [128, 512], F32))
          for i in range(2)]
    rSP = [Res() for _ in range(4)]
    rTP = [Res() for _ in range(2)]
    rOP = [Res() for _ in range(2)]
    r_ssb = [Res() for _ in range(4)]
    r_pbf = [Res() for _ in range(4)]
    r_pts = [Res() for _ in range(4)]
    r_nm = [Res() for _ in range(4)]
    r_rcp = [Res() for _ in range(4)]

    qE.wait(semA, q_done_a)
    qE.wait(semV, cV.n)
    qV.wait(semG, g_masks)

    for k in range(8):
        qG.dma(wctb_h[2048 * k:2048 * (k + 1), :],
               wct_h[2048 * k:2048 * (k + 1), :], cGc)
    g_conv = cGc.n

    s_ready = {}
    norm_ready = {}
    tr_ready = {}
    pt_ready = {}
    pv_ready = {}
    exp_after = {}

    def task_tuple(t):
        return t // NH, (t % NH) // 2, t % 2

    p3_qts = NQT
    p3_beta = True
    p3_level = 4
    if stop is not None and stop.startswith("p3") and ":" in stop:
        kind, narg = stop.split(":")
        if kind == "p3l":
            p3_qts = 1
            p3_beta = False
            p3_level = int(narg)
        else:
            p3_qts = int(narg)
            p3_beta = (kind != "p3s")
            if not p3_beta:
                p3_level = 3
    if stop == "p3z":
        p3_qts = 0
    if stop is not None and stop.startswith("p3y"):
        ny = int(stop.split(":")[1]) if ":" in stop else 4
        p3_qts = 0
        for i in range(ny):
            h2 = i % 2
            sbk = i % 4
            rSP[sbk].acq(qE)
            qE.op(lambda e, i=i, h2=h2, sbk=sbk: e.matmul(
                SP[sbk][:, 0:WIN],
                qT[64 * h2:64 * h2 + 64, 128 * (i % 16):128 * (i % 16) + 128],
                kT2[64 * h2:64 * h2 + 64, 0:WIN],
                start=True, stop=True, tile_position=(64 * h2, 0)), cT, 1)
            qV.wait(semT, cT.n)
            qV.op(lambda e, i=i, sbk=sbk: e.tensor_copy(
                s_sb[:, WIN * (i % 2):WIN * (i % 2) + WIN],
                SP[sbk][:, 0:WIN]), cV, 1)
            rSP[sbk].rel(cV)

    red_done = {}
    exp_cnt = {}
    recip_done = {}
    norm_cnt = {}
    total_tasks = p3_qts * NH
    beta_on = p3_beta and p3_level >= 4

    def emit_S(t):
        qt, p, h2 = task_tuple(t)
        sbk = t % 4
        if t >= 4:
            qE.wait(semV, red_done[t - 4])
        qE.op(lambda e, p=p, h2=h2, qt=qt, sbk=sbk: e.matmul(
            SP[sbk][:, 0:WIN],
            qT[64 * h2:64 * h2 + 64, T * p + 128 * qt:T * p + 128 * qt + 128],
            kT2[64 * h2:64 * h2 + 64, 128 * qt:128 * qt + WIN],
            start=True, stop=True, tile_position=(64 * h2, 0)), cT, 1)
        s_ready[t] = cT.n

    def emit_front(t):
        qt = t // NH
        mi = 1 if qt == 0 else (2 if qt == 1 else 0)
        b = t % 4
        qV.wait(semT, s_ready[t])
        if t >= 4:
            qV.wait(semA, exp_cnt.get(t - 4, 0))
        qV.op(lambda e, t=t, b=b, mi=mi: e.tensor_tensor(
            out=s_sb[:, WIN * b:WIN * b + WIN],
            in0=SP[t % 4][:, 0:WIN],
            in1=masks_sb[:, WIN * mi:WIN * mi + WIN], op=ALU.add))
        qV.op(lambda e, t=t, b=b: e.tensor_reduce(
            out=nmax[:, b:b + 1], in_=s_sb[:, WIN * b:WIN * b + WIN],
            axis=AX.X, op=ALU.max, negate=True), cV, 1)
        red_done[t] = cV.n
        qA.wait(semV, red_done[t])
        if t >= 4:
            qA.wait(semV, norm_cnt.get(t - 4, 0))
        qA.op(lambda e, b=b: e.activation(
            out=s_sb[:, WIN * b:WIN * b + WIN],
            in_=s_sb[:, WIN * b:WIN * b + WIN], func=AF.Exp,
            bias=nmax[:, b:b + 1], scale=1.0, accum_out=rsum[:, b:b + 1]))
        qA.op(lambda e, b=b: e.activation(
            out=rcp[:, b:b + 1], in_=rsum[:, b:b + 1], func=AF.Ln))
        qA.op(lambda e, b=b: e.activation(
            out=rcp[:, b:b + 1], in_=rcp[:, b:b + 1], func=AF.Exp, scale=-1.0),
            cA, 1)
        exp_cnt[t] = cA.n


    def emit_norm(t):
        b = t % 4
        qV.wait(semA, exp_cnt[t])
        if t - 4 in tr_ready:
            qV.wait(semT, tr_ready[t - 4])
        qV.op(lambda e, b=b: e.tensor_scalar(
            out=p_bf[:, WIN * b:WIN * b + WIN],
            in0=s_sb[:, WIN * b:WIN * b + WIN],
            scalar1=rcp[:, b:b + 1], scalar2=None, op0=ALU.mult), cV, 1)
        norm_cnt[t] = cV.n

    def emit_tr(t):
        b = t % 4
        tb = t % 2
        qE.wait(semV, norm_cnt[t])
        rTP[tb].acq(qE)
        for j in range(3):
            qE.op(lambda e, tb=tb, b=b, j=j: e.transpose(
                TP[tb][:, 128 * j:128 * j + 128],
                p_bf[:, WIN * b + 128 * j:WIN * b + 128 * j + 128],
                id_bf[:]), cT if j == 2 else None, 1)
        tr_ready[t] = cT.n
        if t % 2 == 0:
            qA.wait(semT, tr_ready[t])
            r_pts[b].acq(qA)
            qA.op(lambda e, tb=tb, b=b: e.activation(
                out=pt_sb[:, WIN * b:WIN * b + WIN],
                in_=TP[tb][:, 0:WIN], func=AF.Copy, bias=0.0, scale=1.0), cA, 1)
            pt_ready[t] = (semA, cA.n)
            rTP[tb].rel(cA)
        else:
            qV.wait(semT, tr_ready[t])
            r_pts[b].acq(qV)
            qV.op(lambda e, tb=tb, b=b: e.tensor_copy(
                pt_sb[:, WIN * b:WIN * b + WIN], TP[tb][:, 0:WIN]), cV, 1)
            pt_ready[t] = (semV, cV.n)
            rTP[tb].rel(cV)

    def emit_pv(t):
        qt_, p, h2 = task_tuple(t)
        b = t % 4
        ob = (t // 2) % 2
        if h2 == 0:
            rOP[ob].acq(qE)
        qE.wait(*pt_ready[t])
        for j in range(3):
            qE.op(lambda e, qt_=qt_, h2=h2, b=b, j=j, ob=ob: e.matmul(
                OP[ob][64 * h2:64 * h2 + 64, 0:128],
                v_tm[:, 64 * (qt_ + j):64 * (qt_ + j) + 64],
                pt_sb[:, WIN * b + 128 * j:WIN * b + 128 * j + 128],
                start=(j == 0), stop=(j == 2), tile_position=(0, 64 * h2)),
                cT if j == 2 else None, 1)
        pv_ready[t] = cT.n
        r_pts[b].rel(cT)
        if h2 == 1:
            qV.wait(semT, pv_ready[t])
            qV.op(lambda e, qt_=qt_, p=p, ob=ob: e.tensor_copy(
                attn[:, T * p + 128 * qt_:T * p + 128 * qt_ + 128],
                OP[ob][:, 0:128]), cV, 1)
            rOP[ob].rel(cV)

    for t in range(min(4, total_tasks)):
        emit_S(t)
    for t in range(total_tasks + 3):
        if t < total_tasks:
            emit_front(t)
            if t + 4 < total_tasks:
                emit_S(t + 4)
        if 0 <= t - 2 < total_tasks and p3_level >= 3:
            emit_norm(t - 2)
            if beta_on:
                emit_tr(t - 2)
        if 0 <= t - 3 < total_tasks and beta_on:
            emit_pv(t - 3)

    attn_done_v = cV.n
    psum_ctx.close()

    if stop is not None and stop.startswith("p3"):
        qG.wait(semV, attn_done_v)
        qG.wait(semA, cA.n)
        if stop in ("p3z",) or stop.startswith("p3y"):
            qG.wait(semT, cT.n)
            _finish_dbg([(kT2[:, 0:2304], 2304)])
        elif stop.startswith("p3s") or stop.startswith("p3l"):
            _finish_dbg([(s_sb[:, 0:2 * WIN], 2 * WIN),
                         (p_bf[:, 0:2 * WIN], 2 * WIN),
                         (nmax[:, 0:2], 2), (rsum[:, 0:2], 2), (rcp[:, 0:2], 2)])
        elif stop.startswith("p3q"):
            qG.wait(semT, cT.n)
            _finish_dbg([(pt_sb[:, 0:2 * WIN], 2 * WIN),
                         (attn[:, 0:7424], 7424)])
        else:
            _finish_dbg([(kT2[:, 0:2304], 2304), (v_tm[:, 0:1152], 1152),
                         (attn[:, 0:4736], 4736)])
        return _emit_and_close(nc, ctx, qE, qV, qA, qG, qS)

    r_rb = [Res(), Res()]
    r_sq = [Res(), Res()]
    r_lnrow = Res()
    r_obuf = [Res(), Res()]

    def emit_ln(ti, out_mode, P0, P1, P2, P3, rP0, rP1, rP2, rP3):
        col = TD * ti
        qA.wait(semV, cV.n)
        rb_ready = {}
        sq_done = {}
        statsmm_t = {}
        sqmm_t = {}
        for c in range(KC):
            b = c % 2
            if c >= 2:
                qA.wait(semT, statsmm_t[c - 2])
                qA.wait(semV, sq_done[c - 2])
            qA.op(lambda e, c=c, b=b, col=col: e.activation(
                out=rb_buf[:, TD * b:TD * b + TD],
                in_=x32[:, T * c + col:T * c + col + TD],
                func=AF.Copy, bias=0.0, scale=1.0), cA, 1)
            rb_ready[c] = cA.n
            qV.wait(semA, rb_ready[c])
            if c >= 2:
                qV.wait(semT, sqmm_t[c - 2])
            qV.op(lambda e, b=b: e.tensor_tensor(
                out=sq_buf[:, TD * b:TD * b + TD],
                in0=rb_buf[:, TD * b:TD * b + TD],
                in1=rb_buf[:, TD * b:TD * b + TD], op=ALU.mult), cV, 1)
            sq_done[c] = cV.n
            qE.wait(semA, rb_ready[c])
            if c == 0:
                rP0.acq(qE)
            qE.op(lambda e, c=c, b=b: e.matmul(
                P0[0:1, 0:TD], ones_bf[:], rb_buf[:, TD * b:TD * b + TD],
                start=(c == 0), stop=(c == KC - 1)), cT, 1)
            statsmm_t[c] = cT.n
            qE.wait(semV, sq_done[c])
            if c == 0:
                rP1.acq(qE)
            qE.op(lambda e, c=c, b=b: e.matmul(
                P1[0:1, 0:TD], ones_bf[:], sq_buf[:, TD * b:TD * b + TD],
                start=(c == 0), stop=(c == KC - 1)), cT, 1)
            sqmm_t[c] = cT.n
        t_stats = cT.n
        qV.wait(semT, t_stats)
        r_lnrow.acq(qV)
        qV.op(lambda e: e.tensor_scalar(
            out=lnrow[:, 0:TD], in0=P0[0:1, 0:TD], scalar1=1.0 / D, scalar2=None,
            op0=ALU.mult))
        qV.op(lambda e: e.tensor_scalar(
            out=lnrow[:, TD:2 * TD], in0=P1[0:1, 0:TD], scalar1=1.0 / D,
            scalar2=None, op0=ALU.mult), cV, 1)
        rP0.rel(cV)
        rP1.rel(cV)
        qV.op(lambda e: e.tensor_tensor(
            out=lnrow[:, 2 * TD:3 * TD], in0=lnrow[:, 0:TD], in1=lnrow[:, 0:TD],
            op=ALU.mult))
        qV.op(lambda e: e.tensor_tensor(
            out=lnrow[:, TD:2 * TD], in0=lnrow[:, TD:2 * TD],
            in1=lnrow[:, 2 * TD:3 * TD], op=ALU.subtract), cV, 1)
        v_var = cV.n
        qA.wait(semV, v_var)
        qA.op(lambda e: e.activation(
            out=lnrow[:, TD:2 * TD], in_=lnrow[:, TD:2 * TD], func=AF.Ln,
            bias=eps_sb[:], scale=1.0))
        qA.op(lambda e: e.activation(
            out=lnrow[:, TD:2 * TD], in_=lnrow[:, TD:2 * TD], func=AF.Exp,
            scale=-0.5), cA, 1)
        a_rstd = cA.n
        qV.wait(semA, a_rstd)
        qV.op(lambda e: e.tensor_tensor(
            out=lnrow[:, 3 * TD:4 * TD], in0=lnrow[:, 0:TD],
            in1=lnrow[:, TD:2 * TD], op=ALU.mult), cV, 1)
        v_mr = cV.n
        qE.wait(semV, v_mr)
        rP2.acq(qE)
        qE.op(lambda e: e.matmul(
            P2[:, 0:TD], ones_f[:], lnrow[:, TD:2 * TD], start=True, stop=True),
            cT, 1)
        rP3.acq(qE)
        qE.op(lambda e: e.matmul(
            P3[:, 0:TD], ones_f[:], lnrow[:, 3 * TD:4 * TD], start=True, stop=True),
            cT, 1)
        t_bc = cT.n
        r_lnrow.rel(cT)
        qV.wait(semT, t_bc)
        for c in range(KC):
            if out_mode == "x32":
                qV.op(lambda e, c=c, col=col: e.tensor_tensor(
                    out=x32[:, T * c + col:T * c + col + TD],
                    in0=x32[:, T * c + col:T * c + col + TD],
                    in1=P2[:, 0:TD], op=ALU.mult))
                qV.op(lambda e, c=c, col=col: e.tensor_tensor(
                    out=x32[:, T * c + col:T * c + col + TD],
                    in0=x32[:, T * c + col:T * c + col + TD],
                    in1=P3[:, 0:TD], op=ALU.subtract), cV, 1)
            else:
                b = c % 2
                r_obuf[b].acq(qV)
                qV.op(lambda e, c=c, col=col, b=b: e.tensor_tensor(
                    out=o_buf[:, TD * b:TD * b + TD],
                    in0=x32[:, T * c + col:T * c + col + TD],
                    in1=P2[:, 0:TD], op=ALU.mult))
                qV.op(lambda e, b=b: e.tensor_tensor(
                    out=o_buf[:, TD * b:TD * b + TD],
                    in0=o_buf[:, TD * b:TD * b + TD],
                    in1=P3[:, 0:TD], op=ALU.subtract), cV, 1)
                qS.wait(semV, cV.n)
                qS.dma(out_h[128 * c:128 * c + 128, col:col + TD],
                       o_buf[:, TD * b:TD * b + TD], cO2[b])
                r_obuf[b].rel(cO2[b])
        rP2.rel(cV)
        rP3.rel(cV)
        return cV.n

    psum_ctx = ExitStack()
    FP = [psum_ctx.enter_context(nc.psum_tensor(f"fp{i}", [128, 512], F32))
          for i in range(8)]
    rFP = [Res() for _ in range(8)]
    r_wf = [Res(), Res()]
    wf_ready = {}

    def load_wf(ti, k):
        b = k % 2
        r_wf[b].acq(qG)
        qG.dma(wf_buf[:, 1024 * b:1024 * b + 1024],
               wft_h[128 * k:128 * k + 128, :], cW[b])
        wf_ready[(ti, k)] = (cW[b].sem, cW[b].n)

    qE.wait(semV, attn_done_v)
    qE.wait(semA, cA.n)
    qV.wait(semS, cS.n)

    qG.wait(semT, t_vtr)
    load_wf(0, 0)
    load_wf(0, 1)
    ln1_done = {}
    for ti in range(NT):
        for k in range(KC):
            b = k % 2
            qE.wait(*wf_ready[(ti, k)])
            for m in range(KC):
                if k == 0:
                    rFP[m].acq(qE)
                qE.op(lambda e, k=k, m=m, b=b, ti=ti: e.matmul(
                    FP[m][:, 0:TD],
                    wf_buf[:, 1024 * b + 128 * m:1024 * b + 128 * m + 128],
                    attn[:, T * k + TD * ti:T * k + TD * ti + TD],
                    start=(k == 0), stop=(k == KC - 1)),
                    cT if m == KC - 1 else None, 1)
            r_wf[b].rel(cT)
            if k + 2 < KC:
                load_wf(ti, k + 2)
            elif ti + 1 < NT:
                load_wf(ti + 1, k + 2 - KC)
        t_wf = cT.n
        qV.wait(semT, t_wf)
        for m in range(KC):
            qV.op(lambda e, m=m, ti=ti: e.scalar_tensor_tensor(
                out=x32[:, T * m + TD * ti:T * m + TD * ti + TD],
                in0=FP[m][:, 0:TD], scalar=bf_sb[:, m:m + 1],
                in1=x32[:, T * m + TD * ti:T * m + TD * ti + TD],
                op0=ALU.add, op1=ALU.add), cV, 1)
            rFP[m].rel(cV)
        if stop == "p4r":
            ln1_done[ti] = cV.n
        else:
            ln1_done[ti] = emit_ln(ti, "x32", FP[0], FP[1], FP[2], FP[3],
                                   rFP[0], rFP[1], rFP[2], rFP[3])
    psum_ctx.close()

    if stop in ("p4", "p4r"):
        qG.wait(semV, cV.n)
        _finish_dbg([(x32[:, 0:8192], 8192)])
        return _emit_and_close(nc, ctx, qE, qV, qA, qG, qS)

    psum_ctx = ExitStack()
    CP = [psum_ctx.enter_context(nc.psum_tensor(f"cp{i}", [128, 512], F32))
          for i in range(8)]
    rCP = [Res() for _ in range(8)]
    r_wc = [Res(), Res()]
    r_x2b = Res()
    r_yq = [Res() for _ in range(8)]
    r_hq = Res()

    qE.wait(semV, cV.n)
    qE.wait(semA, cA.n)

    wc_seq = [(ti, qq, h, half2) for ti in range(NT) for qq in range(4)
              for h in range(4) for half2 in range(2)]
    wc_ready = {}

    def load_wc(i):
        ti, qq, h, half2 = wc_seq[i]
        b = i % 2
        r_wc[b].acq(qG)
        if i == 0 or i == 1:
            qG.wait(cGc.sem, g_conv)
        row0 = 128 * (32 * h + 8 * qq + 4 * half2)
        qG.dma(wc_buf[:, 4096 * b:4096 * b + 4096].rearrange(
                   "p (c m) -> p c m", c=4),
               wctb_h[row0:row0 + 512, :].rearrange("(c p) m -> p c m", p=128),
               cC[b])
        wc_ready[i] = (cC[b].sem, cC[b].n, b)

    load_wc(0)
    load_wc(1)
    wc_i = 0

    gelu_ready = {}
    pending_ln = [None]
    for ti in range(NT):
        col = TD * ti
        qA.wait(semV, ln1_done[ti])
        r_x2b.acq(qA)
        r_x2b.acq(qV)
        for c in range(KC):
            if c % 2 == 0:
                qA.op(lambda e, c=c, col=col: e.activation(
                    out=x2b[:, TD * c:TD * c + TD],
                    in_=x32[:, T * c + col:T * c + col + TD],
                    func=AF.Copy, bias=0.0, scale=1.0), cA, 1)
            else:
                qV.op(lambda e, c=c, col=col: e.tensor_copy(
                    x2b[:, TD * c:TD * c + TD],
                    x32[:, T * c + col:T * c + col + TD]), cV, 1)
        x2b_a, x2b_v = cA.n, cV.n

        for qq in range(4):
            qE.wait(semA, x2b_a)
            qE.wait(semV, x2b_v)
            emitted_ln = False
            y_ready = {}
            for rpp in range(8):
                rp = 8 * qq + rpp
                g4 = rp % 4
                bank = rpp % 2
                rCP[bank].acq(qE)
                qE.op(lambda e, rp=rp, g4=g4, bank=bank: e.matmul(
                    CP[bank][:, 0:TD],
                    mka_sb[32 * g4:32 * g4 + 32,
                           128 * (rp // 4):128 * (rp // 4) + 128],
                    x2b[32 * g4:32 * g4 + 32, TD * (rp // 4):TD * (rp // 4) + TD],
                    start=True, stop=True, tile_position=(32 * g4, 0)), cT, 1)
                qV.wait(semT, cT.n)
                r_yq[rpp].acq(qV)
                qV.op(lambda e, rpp=rpp, bank=bank: e.tensor_copy(
                    y_q[:, TD * rpp:TD * rpp + TD], CP[bank][:, 0:TD]), cV, 1)
                y_ready[rpp] = cV.n
                rCP[bank].rel(cV)
            if qq == 0 and pending_ln[0] is not None:
                emit_ln(pending_ln[0], "store", CP[6], CP[7], CP[6], CP[7],
                        rCP[6], rCP[7], rCP[6], rCP[7])
                pending_ln[0] = None
            first_gelu = True
            for rpp in range(8):
                rp = 8 * qq + rpp
                qE.wait(semV, y_ready[rpp])
                for h in range(4):
                    bank = 2 + h
                    rCP[bank].acq(qE)
                    qE.op(lambda e, rp=rp, rpp=rpp, h=h, bank=bank: e.matmul(
                        CP[bank][:, 0:TD],
                        upb_sb[32 * h:32 * h + 32, 128 * rp:128 * rp + 128],
                        y_q[32 * h:32 * h + 32, TD * rpp:TD * rpp + TD],
                        start=True, stop=True, tile_position=(32 * h, 0)), cT, 1)
                    t_z = cT.n
                    qA.wait(semT, t_z)
                    if first_gelu:
                        r_hq.acq(qA)
                        first_gelu = False
                    lidx = 8 * h + rpp
                    qA.op(lambda e, lidx=lidx, bank=bank: e.activation(
                        out=h_q[:, TD * lidx:TD * lidx + TD],
                        in_=CP[bank][:, 0:TD], func=AF.Gelu), cA, 1)
                    gelu_ready[lidx] = cA.n
                    rCP[bank].rel(cA)
                r_yq[rpp].rel(cT)
            first = True
            for h in range(4):
                for half2 in range(2):
                    gsem, gn, wb = wc_ready[wc_i]
                    qE.wait(gsem, gn)
                    for cc in range(4):
                        rpg = 4 * half2 + cc
                        lidx = 8 * h + rpg
                        qE.wait(semA, gelu_ready[lidx])
                        if first:
                            for m in range(KC):
                                rCP[m].acq(qE)
                        for m in range(KC):
                            qE.op(lambda e, wb=wb, cc=cc, m=m, lidx=lidx,
                                  st=first, sp=(h == 3 and rpg == 7): e.matmul(
                                CP[m][:, 0:TD],
                                wc_buf[:, 4096 * wb + 1024 * cc + 128 * m:
                                       4096 * wb + 1024 * cc + 128 * m + 128],
                                h_q[:, TD * lidx:TD * lidx + TD],
                                start=st, stop=sp),
                                cT if (m == KC - 1 and cc == 3) else None, 1)
                        first = False
                    r_wc[wb].rel(cT)
                    if wc_i + 2 < len(wc_seq):
                        load_wc(wc_i + 2)
                    wc_i += 1
            t_cproj = cT.n
            r_hq.rel(cT)
            qV.wait(semT, t_cproj)
            for m in range(KC):
                if qq == 0:
                    qV.op(lambda e, m=m: e.tensor_copy(
                        acc[:, TD * m:TD * m + TD], CP[m][:, 0:TD]), cV, 1)
                else:
                    qV.op(lambda e, m=m: e.tensor_tensor(
                        out=acc[:, TD * m:TD * m + TD],
                        in0=acc[:, TD * m:TD * m + TD],
                        in1=CP[m][:, 0:TD], op=ALU.add), cV, 1)
                rCP[m].rel(cV)
        for c in range(KC):
            qV.op(lambda e, c=c, col=col: e.tensor_tensor(
                out=x32[:, T * c + col:T * c + col + TD],
                in0=x32[:, T * c + col:T * c + col + TD],
                in1=acc[:, TD * c:TD * c + TD], op=ALU.add), cV, 1)
        r_x2b.fa = (semT, cT.n)
        pending_ln[0] = ti
    emit_ln(pending_ln[0], "store", CP[6], CP[7], CP[6], CP[7],
            rCP[6], rCP[7], rCP[6], rCP[7])

    qS.wait(cO2[0].sem, cO2[0].n)
    qS.wait(cO2[1].sem, cO2[1].n)
    return _emit_and_close(nc, ctx, qE, qV, qA, qG, qS)


def _emit_and_close(nc, ctx, qE, qV, qA, qG, qS):
    with nc.Block() as block:
        @block.tensor
        def _(e):
            qE.emit(e)

        @block.vector
        def _(e):
            qV.emit(e)

        @block.scalar
        def _(e):
            qA.emit(e)

        @block.gpsimd
        def _(e):
            qG.emit(e)

        @block.sync
        def _(e):
            qS.emit(e)

    ctx.close()
    return nc



def _host_prep(inputs):
    _ = inputs.get("L0_attn_segment_pos")
    x = np.asarray(inputs["L0_attn_x"], dtype=np.float32)
    Wq = np.asarray(inputs["L0_attn_Wq"], dtype=np.float32)
    Wk = np.asarray(inputs["L0_attn_Wk"], dtype=np.float32)
    Wv = np.asarray(inputs["L0_attn_Wv"], dtype=np.float32)
    Wf = np.asarray(inputs["L0_attn_Wf"], dtype=np.float32)
    bfv = np.asarray(inputs["L0_attn_bf"], dtype=np.float32)
    mk = np.asarray(inputs["L0_ffn_rail_maker"], dtype=np.float32)
    up = np.asarray(inputs["L0_ffn_rail_up_proj"], dtype=np.float32)
    Wc = np.asarray(inputs["L0_ffn_w_cproj"], dtype=np.float32)

    shared = {}
    shared["wqt"] = np.ascontiguousarray(Wq.T)
    shared["wkd"] = np.ascontiguousarray(np.concatenate([Wk.T, Wk.T], axis=1))
    shared["wvt"] = np.ascontiguousarray(Wv.T)
    shared["wft"] = np.ascontiguousarray(Wf.T)
    shared["wct"] = np.ascontiguousarray(Wc.T)
    shared["bfr"] = np.ascontiguousarray(bfv.reshape(KC, 128).T)
    shared["ident"] = np.eye(128, dtype=np.float32)

    mka = np.zeros((128, 1024), dtype=np.float32)
    h_, r_, i_, j_ = np.indices((4, 64, 16, 16))
    rp, rr = r_ // 2, r_ % 2
    p = 32 * (rp % 4) + 16 * rr + i_
    cc = 128 * (rp // 4) + 32 * h_ + 16 * rr + j_
    mka[p.ravel(), cc.ravel()] = mk.ravel()
    shared["mka"] = mka

    upb = np.zeros((128, 4096), dtype=np.float32)
    h_, r_, k_, j_ = np.indices((4, 64, 64, 16))
    rp, rr = r_ // 2, r_ % 2
    p = 32 * h_ + 16 * rr + j_
    cc = 128 * rp + 64 * rr + k_
    upb[p.ravel(), cc.ravel()] = up.ravel()
    shared["upb"] = upb

    i_g, j_g = np.indices((128, WIN))
    band = (j_g >= i_g) & (j_g <= i_g + 256)
    neg = np.float32(-1e30)
    m0 = np.where(band, np.float32(0.0), neg).astype(np.float32)
    m1s = np.where(band & (j_g >= 256), np.float32(0.0), neg)
    m2s = np.where(band & (j_g >= 128), np.float32(0.0), neg)
    masks_plain = np.ascontiguousarray(np.concatenate([m0, m0, m0], axis=1))
    masks_start = np.ascontiguousarray(
        np.concatenate([m0, m1s, m2s], axis=1).astype(np.float32))

    in_maps = []
    for core in range(8):
        b = core // 4
        r0 = (core % 4) * T
        xs = np.zeros((TH, D), dtype=np.float32)
        lo = max(0, r0 - HALO)
        xs[HALO - (r0 - lo):] = x[b, lo:r0 + T]
        d = dict(shared)
        d["x"] = np.ascontiguousarray(xs.T)
        d["masks"] = masks_start if r0 == 0 else masks_plain
        in_maps.append(d)
    return in_maps


def kernel(**inputs):
    in_maps = _host_prep(inputs)
    nc = build_program()
    res = run_bass_kernel_spmd(nc, in_maps, core_ids=list(range(8)))
    out = np.empty((2, 8192, D), dtype=np.float32)
    for core in range(8):
        b = core // 4
        r0 = (core % 4) * T
        out[b, r0:r0 + T, :] = res.results[core]["out"].T
    return out


if __name__ == "__main__":
    nc = build_program()
    print("built ok")


# version 12
# speedup vs baseline: 1.0646x; 1.0015x over previous
import numpy as np
import ml_dtypes
from contextlib import ExitStack

import concourse.bass as bass
import concourse.mybir as mybir
from concourse.bass_utils import run_bass_kernel_spmd

F32 = mybir.dt.float32
BF16 = mybir.dt.bfloat16
AF = mybir.ActivationFunctionType
ALU = mybir.AluOpType
AX = mybir.AxisListType
BT = ml_dtypes.bfloat16

D = 1024
KC = 8
T = 2048
HALO = 256
TH = T + HALO
TD = 512
NT = T // TD
NQT = T // 128
WIN = 384
NH = 16
HD = 64
FF = 16384
EPS = 1e-5


class Ctr:
    def __init__(self, sem):
        self.sem = sem
        self.n = 0


class Res:

    def __init__(self):
        self.fa = None

    def acq(self, q):
        if self.fa is not None:
            q.wait(self.fa[0], self.fa[1])

    def rel(self, ctr):
        self.fa = (ctr.sem, ctr.n)


class Q:

    def __init__(self, name):
        self.name = name
        self.items = []

    def wait(self, sem, n):
        if n > 0:
            self.items.append(("w", sem, n))

    def op(self, fn, inc=None, k=1):
        self.items.append(("o", fn, None if inc is None else inc.sem, k))
        if inc is not None:
            inc.n += k

    def dma(self, out_ap, in_ap, inc):
        self.op(lambda e, o=out_ap, i=in_ap: e.dma_start(out=o, in_=i), inc, 16)

    def emit(self, eng):
        for it in self.items:
            if it[0] == "w":
                eng.wait_ge(it[1], it[2])
            else:
                _, fn, sem, k = it
                inst = fn(eng)
                if sem is not None:
                    inst.then_inc(sem, k)


def build_program(stop=None):
    nc = bass.Bass("TRN2", target_bir_lowering=False)

    x_h = nc.dram_tensor("x", [D, TH], F32, kind="ExternalInput").ap()
    wqt_h = nc.dram_tensor("wqt", [D, D], F32, kind="ExternalInput").ap()
    wkd_h = nc.dram_tensor("wkd", [D, 128], F32, kind="ExternalInput").ap()
    wvt_h = nc.dram_tensor("wvt", [D, HD], F32, kind="ExternalInput").ap()
    wft_h = nc.dram_tensor("wft", [D, D], F32, kind="ExternalInput").ap()
    wct_h = nc.dram_tensor("wct", [FF, D], F32, kind="ExternalInput").ap()
    bfr_h = nc.dram_tensor("bfr", [128, KC], F32, kind="ExternalInput").ap()
    mka_h = nc.dram_tensor("mka", [128, 1024], F32, kind="ExternalInput").ap()
    upb_h = nc.dram_tensor("upb", [128, 4096], F32, kind="ExternalInput").ap()
    masks_h = nc.dram_tensor("masks", [128, 3 * WIN], F32, kind="ExternalInput").ap()
    ident_h = nc.dram_tensor("ident", [128, 128], F32, kind="ExternalInput").ap()
    out_h = nc.dram_tensor("out", [D, T], F32, kind="ExternalOutput").ap()
    wctb_h = nc.dram_tensor("wctb", [FF, D], BF16, kind="Internal").ap()
    dbg_h = nc.dram_tensor("dbg", [128, 8192], F32, kind="ExternalOutput").ap()

    ctx = ExitStack()
    sb = lambda n, s, d: ctx.enter_context(nc.sbuf_tensor(n, s, d))

    x32 = sb("x32", [128, KC * T], F32)
    kT2 = sb("kT2", [128, TH], BF16)
    v_tm = sb("v_tm", [128, (TH // 128) * HD], BF16)
    mka_sb = sb("mka_sb", [128, 1024], BF16)
    upb_sb = sb("upb_sb", [128, 4096], BF16)
    bf_sb = sb("bf_sb", [128, KC], F32)
    id_bf = sb("id_bf", [128, 128], BF16)
    ones_bf = sb("ones_bf", [128, 1], BF16)
    ones_f = sb("ones_f", [1, 128], F32)
    wc_buf = sb("wc_buf", [128, 2 * 4 * 1024], BF16)
    lnrow = sb("lnrow", [1, 4 * TD], F32)
    rb_buf = sb("rb_buf", [128, 2 * TD], BF16)
    eps_sb = sb("eps_sb", [1, 1], F32)
    masks_sb = sb("masks_sb", [128, 3 * WIN], BF16)
    sq_buf = sb("sq_buf", [128, 2 * TD], BF16)

    arA = bass.AutoArena(nc)
    xt_buf = arA.sbuf_tensor("xt_buf", [128, 2 * KC * TD], BF16)
    wq_sb = arA.sbuf_tensor("wq_sb", [128, KC * 1024], BF16)
    wkd_sb = arA.sbuf_tensor("wkd_sb", [128, KC * 128], BF16)
    wvt_sb = arA.sbuf_tensor("wvt_sb", [128, KC * HD], BF16)
    v_fm = arA.sbuf_tensor("v_fm", [64, TH], BF16)

    arQ = bass.AutoArena(nc)
    qT = arQ.sbuf_tensor("qT", [128, KC * T], BF16)

    arB = bass.AutoArena(nc)
    attn = arB.sbuf_tensor("attn", [128, KC * T], BF16)
    s_sb = arB.sbuf_tensor("s_sb", [128, 6 * WIN], F32)
    p_bf = arB.sbuf_tensor("p_bf", [128, 6 * WIN], BF16)
    pt_sb = arB.sbuf_tensor("pt_sb", [128, 6 * WIN], BF16)
    nmax = arB.sbuf_tensor("nmax", [128, 6], F32)
    rsum = arB.sbuf_tensor("rsum", [128, 6], F32)
    rcp = arB.sbuf_tensor("rcp", [128, 6], F32)
    wf_buf = arB.sbuf_tensor("wf_buf", [128, 2 * 1024], BF16)

    arC = bass.AutoArena(nc)
    x2b = arC.sbuf_tensor("x2b", [128, KC * TD], BF16)
    y_q = arC.sbuf_tensor("y_q", [128, 8 * TD], BF16)
    h_q = arC.sbuf_tensor("h_q", [128, 32 * TD], BF16)
    acc = arC.sbuf_tensor("acc", [128, KC * TD], F32)
    o_buf = arC.sbuf_tensor("o_buf", [128, 2 * TD], F32)

    arA.allocate(side="left")
    arQ.allocate(side="right")
    arA.free()
    arB.allocate(side="left")
    arB.free()
    arQ.free()
    arC.allocate(side="left")
    arC.free()

    semS = ctx.enter_context(nc.semaphore("semS"))
    semG = ctx.enter_context(nc.semaphore("semG"))
    semT = ctx.enter_context(nc.semaphore("semT"))
    semV = ctx.enter_context(nc.semaphore("semV"))
    semA = ctx.enter_context(nc.semaphore("semA"))
    semO = ctx.enter_context(nc.semaphore("semO"))
    cS, cG = Ctr(semS), Ctr(semG)
    cT, cV, cA, cO = Ctr(semT), Ctr(semV), Ctr(semA), Ctr(semO)
    cX = [Ctr(ctx.enter_context(nc.semaphore(f"semX{i}"))) for i in range(2)]
    cW = [Ctr(ctx.enter_context(nc.semaphore(f"semW{i}"))) for i in range(2)]
    cC = [Ctr(ctx.enter_context(nc.semaphore(f"semC{i}"))) for i in range(2)]
    cO2 = [Ctr(ctx.enter_context(nc.semaphore(f"semO{i}"))) for i in range(2)]
    cGc = Ctr(ctx.enter_context(nc.semaphore("semGc")))

    qE, qV, qA, qG, qS = Q("tensor"), Q("vector"), Q("scalar"), Q("gpsimd"), Q("sync")

    psum_ctx = ExitStack()

    qS.dma(x32[:].rearrange("p (k t) -> p k t", k=KC),
           x_h[:, HALO:TH].rearrange("(k p) t -> p k t", p=128), cS)
    qS.dma(bf_sb[:], bfr_h, cS)

    qG.dma(id_bf[:], ident_h, cG)
    g_id = cG.n
    qG.dma(masks_sb[:], masks_h, cG)
    g_masks = cG.n
    qG.dma(wkd_sb[:].rearrange("p (k m) -> p k m", k=KC),
           wkd_h.rearrange("(k p) m -> p k m", p=128), cG)
    qG.dma(wvt_sb[:].rearrange("p (k m) -> p k m", k=KC),
           wvt_h.rearrange("(k p) m -> p k m", p=128), cG)
    qG.dma(mka_sb[:], mka_h, cG)
    qG.dma(upb_sb[:], upb_h, cG)
    qG.dma(wq_sb[:].rearrange("p (k m) -> p k m", k=KC),
           wqt_h.rearrange("(k p) m -> p k m", p=128), cG)
    qG.op(lambda e: e.memset(ones_bf[:], 1.0))
    qG.op(lambda e: e.memset(eps_sb[:], EPS))
    qG.op(lambda e: e.memset(ones_f[:], 1.0), cG, 1)
    g_w0 = cG.n

    kv_tiles = [(i * TD, TD) for i in range(4)] + [(4 * TD, HALO)]
    q_tiles = [(HALO + i * TD, TD) for i in range(NT)]
    all_tiles = kv_tiles + q_tiles
    xt_ready = [None] * len(all_tiles)
    xt_res = [Res(), Res()]

    def load_xt(bi):
        c0, w = all_tiles[bi]
        half = (bi % 2) * KC * TD
        xt_res[bi % 2].acq(qG)
        qG.dma(xt_buf[:, half:half + KC * w].rearrange(
                   "p (k t) -> p k t", k=KC),
               x_h[:, c0:c0 + w].rearrange("(k p) t -> p k t", p=128), cX[bi % 2])
        xt_ready[bi] = (cX[bi % 2].sem, cX[bi % 2].n)

    load_xt(0)
    load_xt(1)

    KV0s = [psum_ctx.enter_context(nc.psum_tensor(f"kv0{i}", [128, 512], F32))
            for i in range(2)]
    KV1s = [psum_ctx.enter_context(nc.psum_tensor(f"kv1{i}", [128, 512], F32))
            for i in range(2)]
    QP = [psum_ctx.enter_context(nc.psum_tensor(f"qp{i}", [128, 512], F32))
          for i in range(2)]
    VT = [psum_ctx.enter_context(nc.psum_tensor(f"vt{i}", [128, 1024], BF16))
          for i in range(2)]
    rKV0s = [Res(), Res()]
    rKV1s = [Res(), Res()]
    rQP = [Res(), Res()]
    rVT = [Res(), Res()]

    qE.wait(semG, g_w0)

    for bi, (c0, w) in enumerate(kv_tiles):
        half = (bi % 2) * KC * TD
        pb = bi % 2
        KV0, KV1 = KV0s[pb], KV1s[pb]
        qE.wait(*xt_ready[bi])
        rKV0s[pb].acq(qE)
        for k in range(KC):
            qE.op(lambda e, k=k, h=half, w=w, KV0=KV0: e.matmul(
                KV0[:, 0:w], wkd_sb[:, 128 * k:128 * k + 128],
                xt_buf[:, h + k * w:h + k * w + w],
                start=(k == 0), stop=(k == KC - 1)),
                cT if k == KC - 1 else None, 1)
        t_k = cT.n
        rKV1s[pb].acq(qE)
        for k in range(KC):
            qE.op(lambda e, k=k, h=half, w=w, KV1=KV1: e.matmul(
                KV1[0:64, 0:w], wvt_sb[:, 64 * k:64 * k + 64],
                xt_buf[:, h + k * w:h + k * w + w],
                start=(k == 0), stop=(k == KC - 1)),
                cT if k == KC - 1 else None, 1)
        t_v = cT.n
        xt_res[bi % 2].rel(cT)
        if bi + 2 < len(all_tiles):
            load_xt(bi + 2)
        qV.wait(semT, t_k)
        qV.op(lambda e, c0=c0, w=w, KV0=KV0: e.tensor_copy(
            kT2[:, c0:c0 + w], KV0[:, 0:w]), cV, 1)
        rKV0s[pb].rel(cV)
        qV.wait(semT, t_v)
        qV.op(lambda e, c0=c0, w=w, KV1=KV1: e.tensor_copy(
            v_fm[:, c0:c0 + w], KV1[0:64, 0:w]), cV, 1)
        rKV1s[pb].rel(cV)
    v_fm_ready = cV.n

    qE.wait(semV, v_fm_ready)
    for st in range(TH // 128):
        b = st % 2
        rVT[b].acq(qE)
        qE.op(lambda e, st=st, b=b: e.transpose(
            VT[b][:, 0:64], v_fm[:, 128 * st:128 * st + 128], id_bf[0:64, 0:64]),
            cT, 1)
        qV.wait(semT, cT.n)
        qV.op(lambda e, st=st, b=b: e.tensor_copy(
            v_tm[:, 64 * st:64 * st + 64], VT[b][:, 0:64]), cV, 1)
        rVT[b].rel(cV)
    v_tm_ready = cV.n
    t_vtr = cT.n

    for ti in range(NT):
        bi = 5 + ti
        half = (bi % 2) * KC * TD
        qE.wait(*xt_ready[bi])
        for m in range(KC):
            pb = m % 2
            rQP[pb].acq(qE)
            for k in range(KC):
                qE.op(lambda e, k=k, m=m, h=half, pb=pb: e.matmul(
                    QP[pb][:, 0:TD],
                    wq_sb[:, 1024 * k + 128 * m:1024 * k + 128 * m + 128],
                    xt_buf[:, h + k * TD:h + k * TD + TD],
                    start=(k == 0), stop=(k == KC - 1)),
                    cT if k == KC - 1 else None, 1)
            qA.wait(semT, cT.n)
            qA.op(lambda e, m=m, ti=ti, pb=pb: e.activation(
                out=qT[:, T * m + TD * ti:T * m + TD * ti + TD],
                in_=QP[pb][:, 0:TD], func=AF.Copy, bias=0.0, scale=0.125), cA, 1)
            rQP[pb].rel(cA)
        xt_res[bi % 2].rel(cT)
        if bi + 2 < len(all_tiles):
            load_xt(bi + 2)
    q_done_a = cA.n
    psum_ctx.close()

    def _finish_dbg(srcs):
        off = 0
        for ap, wdt in srcs:
            qG.dma(dbg_h[:, off:off + wdt], ap, cO)
            off += wdt
        qG.wait(semO, cO.n)

    if stop == "p1":
        qG.wait(semV, v_tm_ready)
        _finish_dbg([(kT2[:, 0:2304], 2304), (v_tm[:, 0:1152], 1152)])
        return _emit_and_close(nc, ctx, qE, qV, qA, qG, qS)
    if stop == "p2":
        qG.wait(semA, q_done_a)
        _finish_dbg([(qT[:, 0:8192], 8192)])
        return _emit_and_close(nc, ctx, qE, qV, qA, qG, qS)

    psum_ctx = ExitStack()
    SP = [psum_ctx.enter_context(nc.psum_tensor(f"sp{i}", [128, 512], F32))
          for i in range(4)]
    TP = [psum_ctx.enter_context(nc.psum_tensor(f"tp{i}", [128, 1024], BF16))
          for i in range(2)]
    OP = [psum_ctx.enter_context(nc.psum_tensor(f"op{i}", [128, 512], F32))
          for i in range(2)]
    rSP = [Res() for _ in range(4)]
    rTP = [Res() for _ in range(2)]
    rOP = [Res() for _ in range(2)]
    r_ssb = [Res() for _ in range(4)]
    r_pbf = [Res() for _ in range(4)]
    r_pts = [Res() for _ in range(6)]
    r_nm = [Res() for _ in range(4)]
    r_rcp = [Res() for _ in range(4)]

    qE.wait(semA, q_done_a)
    qE.wait(semV, cV.n)
    qV.wait(semG, g_masks)

    for k in range(8):
        qG.dma(wctb_h[2048 * k:2048 * (k + 1), :],
               wct_h[2048 * k:2048 * (k + 1), :], cGc)
    g_conv = cGc.n

    s_ready = {}
    norm_ready = {}
    tr_ready = {}
    pt_ready = {}
    pv_ready = {}
    exp_after = {}

    def task_tuple(t):
        return t // NH, (t % NH) // 2, t % 2

    p3_qts = NQT
    p3_beta = True
    p3_level = 4
    if stop is not None and stop.startswith("p3") and ":" in stop:
        kind, narg = stop.split(":")
        if kind == "p3l":
            p3_qts = 1
            p3_beta = False
            p3_level = int(narg)
        else:
            p3_qts = int(narg)
            p3_beta = (kind != "p3s")
            if not p3_beta:
                p3_level = 3
    if stop == "p3z":
        p3_qts = 0
    if stop is not None and stop.startswith("p3y"):
        ny = int(stop.split(":")[1]) if ":" in stop else 4
        p3_qts = 0
        for i in range(ny):
            h2 = i % 2
            sbk = i % 4
            rSP[sbk].acq(qE)
            qE.op(lambda e, i=i, h2=h2, sbk=sbk: e.matmul(
                SP[sbk][:, 0:WIN],
                qT[64 * h2:64 * h2 + 64, 128 * (i % 16):128 * (i % 16) + 128],
                kT2[64 * h2:64 * h2 + 64, 0:WIN],
                start=True, stop=True, tile_position=(64 * h2, 0)), cT, 1)
            qV.wait(semT, cT.n)
            qV.op(lambda e, i=i, sbk=sbk: e.tensor_copy(
                s_sb[:, WIN * (i % 2):WIN * (i % 2) + WIN],
                SP[sbk][:, 0:WIN]), cV, 1)
            rSP[sbk].rel(cV)

    red_done = {}
    exp_cnt = {}
    recip_done = {}
    norm_cnt = {}
    total_tasks = p3_qts * NH
    beta_on = p3_beta and p3_level >= 4

    def emit_S(t):
        qt, p, h2 = task_tuple(t)
        sbk = t % 4
        if t >= 4:
            qE.wait(semV, red_done[t - 4])
        qE.op(lambda e, p=p, h2=h2, qt=qt, sbk=sbk: e.matmul(
            SP[sbk][:, 0:WIN],
            qT[64 * h2:64 * h2 + 64, T * p + 128 * qt:T * p + 128 * qt + 128],
            kT2[64 * h2:64 * h2 + 64, 128 * qt:128 * qt + WIN],
            start=True, stop=True, tile_position=(64 * h2, 0)), cT, 1)
        s_ready[t] = cT.n

    def emit_front(t):
        qt = t // NH
        mi = 1 if qt == 0 else (2 if qt == 1 else 0)
        b = t % 6
        qV.wait(semT, s_ready[t])
        if t >= 6:
            qV.wait(semA, exp_cnt.get(t - 6, 0))
        qV.op(lambda e, t=t, b=b, mi=mi: e.tensor_tensor(
            out=s_sb[:, WIN * b:WIN * b + WIN],
            in0=SP[t % 4][:, 0:WIN],
            in1=masks_sb[:, WIN * mi:WIN * mi + WIN], op=ALU.add))
        qV.op(lambda e, t=t, b=b: e.tensor_reduce(
            out=nmax[:, b:b + 1], in_=s_sb[:, WIN * b:WIN * b + WIN],
            axis=AX.X, op=ALU.max, negate=True), cV, 1)
        red_done[t] = cV.n
        qA.wait(semV, red_done[t])
        if t >= 6:
            qA.wait(semV, norm_cnt.get(t - 6, 0))
        qA.op(lambda e, b=b: e.activation(
            out=s_sb[:, WIN * b:WIN * b + WIN],
            in_=s_sb[:, WIN * b:WIN * b + WIN], func=AF.Exp,
            bias=nmax[:, b:b + 1], scale=1.0, accum_out=rsum[:, b:b + 1]))
        qA.op(lambda e, b=b: e.activation(
            out=rcp[:, b:b + 1], in_=rsum[:, b:b + 1], func=AF.Ln))
        qA.op(lambda e, b=b: e.activation(
            out=rcp[:, b:b + 1], in_=rcp[:, b:b + 1], func=AF.Exp, scale=-1.0),
            cA, 1)
        exp_cnt[t] = cA.n


    def emit_norm(t):
        b = t % 6
        qV.wait(semA, exp_cnt[t])
        if t - 6 in tr_ready:
            qV.wait(semT, tr_ready[t - 6])
        qV.op(lambda e, b=b: e.tensor_scalar(
            out=p_bf[:, WIN * b:WIN * b + WIN],
            in0=s_sb[:, WIN * b:WIN * b + WIN],
            scalar1=rcp[:, b:b + 1], scalar2=None, op0=ALU.mult), cV, 1)
        norm_cnt[t] = cV.n

    def emit_tr(t):
        b = t % 6
        tb = t % 2
        qE.wait(semV, norm_cnt[t])
        rTP[tb].acq(qE)
        for j in range(3):
            qE.op(lambda e, tb=tb, b=b, j=j: e.transpose(
                TP[tb][:, 128 * j:128 * j + 128],
                p_bf[:, WIN * b + 128 * j:WIN * b + 128 * j + 128],
                id_bf[:]), cT if j == 2 else None, 1)
        tr_ready[t] = cT.n
        if t % 2 == 0:
            qA.wait(semT, tr_ready[t])
            r_pts[b].acq(qA)
            qA.op(lambda e, tb=tb, b=b: e.activation(
                out=pt_sb[:, WIN * b:WIN * b + WIN],
                in_=TP[tb][:, 0:WIN], func=AF.Copy, bias=0.0, scale=1.0), cA, 1)
            pt_ready[t] = (semA, cA.n)
            rTP[tb].rel(cA)
        else:
            qV.wait(semT, tr_ready[t])
            r_pts[b].acq(qV)
            qV.op(lambda e, tb=tb, b=b: e.tensor_copy(
                pt_sb[:, WIN * b:WIN * b + WIN], TP[tb][:, 0:WIN]), cV, 1)
            pt_ready[t] = (semV, cV.n)
            rTP[tb].rel(cV)

    def emit_pv(t):
        qt_, p, h2 = task_tuple(t)
        b = t % 6
        ob = (t // 2) % 2
        if h2 == 0:
            rOP[ob].acq(qE)
        qE.wait(*pt_ready[t])
        for j in range(3):
            qE.op(lambda e, qt_=qt_, h2=h2, b=b, j=j, ob=ob: e.matmul(
                OP[ob][64 * h2:64 * h2 + 64, 0:128],
                v_tm[:, 64 * (qt_ + j):64 * (qt_ + j) + 64],
                pt_sb[:, WIN * b + 128 * j:WIN * b + 128 * j + 128],
                start=(j == 0), stop=(j == 2), tile_position=(0, 64 * h2)),
                cT if j == 2 else None, 1)
        pv_ready[t] = cT.n
        r_pts[b].rel(cT)
        if h2 == 1:
            qV.wait(semT, pv_ready[t])
            qV.op(lambda e, qt_=qt_, p=p, ob=ob: e.tensor_copy(
                attn[:, T * p + 128 * qt_:T * p + 128 * qt_ + 128],
                OP[ob][:, 0:128]), cV, 1)
            rOP[ob].rel(cV)

    for t in range(min(4, total_tasks)):
        emit_S(t)
    for t in range(total_tasks + 3):
        if t < total_tasks:
            emit_front(t)
            if t + 4 < total_tasks:
                emit_S(t + 4)
        if 0 <= t - 2 < total_tasks and p3_level >= 3:
            emit_norm(t - 2)
            if beta_on:
                emit_tr(t - 2)
        if 0 <= t - 3 < total_tasks and beta_on:
            emit_pv(t - 3)

    attn_done_v = cV.n
    psum_ctx.close()

    if stop is not None and stop.startswith("p3"):
        qG.wait(semV, attn_done_v)
        qG.wait(semA, cA.n)
        if stop in ("p3z",) or stop.startswith("p3y"):
            qG.wait(semT, cT.n)
            _finish_dbg([(kT2[:, 0:2304], 2304)])
        elif stop.startswith("p3s") or stop.startswith("p3l"):
            _finish_dbg([(s_sb[:, 0:2 * WIN], 2 * WIN),
                         (p_bf[:, 0:2 * WIN], 2 * WIN),
                         (nmax[:, 0:2], 2), (rsum[:, 0:2], 2), (rcp[:, 0:2], 2)])
        elif stop.startswith("p3q"):
            qG.wait(semT, cT.n)
            _finish_dbg([(pt_sb[:, 0:2 * WIN], 2 * WIN),
                         (attn[:, 0:7424], 7424)])
        else:
            _finish_dbg([(kT2[:, 0:2304], 2304), (v_tm[:, 0:1152], 1152),
                         (attn[:, 0:4736], 4736)])
        return _emit_and_close(nc, ctx, qE, qV, qA, qG, qS)

    r_rb = [Res(), Res()]
    r_sq = [Res(), Res()]
    r_lnrow = Res()
    r_obuf = [Res(), Res()]

    def emit_ln(ti, out_mode, P0, P1, P2, P3, rP0, rP1, rP2, rP3):
        col = TD * ti
        qA.wait(semV, cV.n)
        rb_ready = {}
        sq_done = {}
        statsmm_t = {}
        sqmm_t = {}
        for c in range(KC):
            b = c % 2
            if c >= 2:
                qA.wait(semT, statsmm_t[c - 2])
                qA.wait(semV, sq_done[c - 2])
            qA.op(lambda e, c=c, b=b, col=col: e.activation(
                out=rb_buf[:, TD * b:TD * b + TD],
                in_=x32[:, T * c + col:T * c + col + TD],
                func=AF.Copy, bias=0.0, scale=1.0), cA, 1)
            rb_ready[c] = cA.n
            qV.wait(semA, rb_ready[c])
            if c >= 2:
                qV.wait(semT, sqmm_t[c - 2])
            qV.op(lambda e, b=b: e.tensor_tensor(
                out=sq_buf[:, TD * b:TD * b + TD],
                in0=rb_buf[:, TD * b:TD * b + TD],
                in1=rb_buf[:, TD * b:TD * b + TD], op=ALU.mult), cV, 1)
            sq_done[c] = cV.n
            qE.wait(semA, rb_ready[c])
            if c == 0:
                rP0.acq(qE)
            qE.op(lambda e, c=c, b=b: e.matmul(
                P0[0:1, 0:TD], ones_bf[:], rb_buf[:, TD * b:TD * b + TD],
                start=(c == 0), stop=(c == KC - 1)), cT, 1)
            statsmm_t[c] = cT.n
            qE.wait(semV, sq_done[c])
            if c == 0:
                rP1.acq(qE)
            qE.op(lambda e, c=c, b=b: e.matmul(
                P1[0:1, 0:TD], ones_bf[:], sq_buf[:, TD * b:TD * b + TD],
                start=(c == 0), stop=(c == KC - 1)), cT, 1)
            sqmm_t[c] = cT.n
        t_stats = cT.n
        qV.wait(semT, t_stats)
        r_lnrow.acq(qV)
        qV.op(lambda e: e.tensor_scalar(
            out=lnrow[:, 0:TD], in0=P0[0:1, 0:TD], scalar1=1.0 / D, scalar2=None,
            op0=ALU.mult))
        qV.op(lambda e: e.tensor_scalar(
            out=lnrow[:, TD:2 * TD], in0=P1[0:1, 0:TD], scalar1=1.0 / D,
            scalar2=None, op0=ALU.mult), cV, 1)
        rP0.rel(cV)
        rP1.rel(cV)
        qV.op(lambda e: e.tensor_tensor(
            out=lnrow[:, 2 * TD:3 * TD], in0=lnrow[:, 0:TD], in1=lnrow[:, 0:TD],
            op=ALU.mult))
        qV.op(lambda e: e.tensor_tensor(
            out=lnrow[:, TD:2 * TD], in0=lnrow[:, TD:2 * TD],
            in1=lnrow[:, 2 * TD:3 * TD], op=ALU.subtract), cV, 1)
        v_var = cV.n
        qA.wait(semV, v_var)
        qA.op(lambda e: e.activation(
            out=lnrow[:, TD:2 * TD], in_=lnrow[:, TD:2 * TD], func=AF.Ln,
            bias=eps_sb[:], scale=1.0))
        qA.op(lambda e: e.activation(
            out=lnrow[:, TD:2 * TD], in_=lnrow[:, TD:2 * TD], func=AF.Exp,
            scale=-0.5), cA, 1)
        a_rstd = cA.n
        qV.wait(semA, a_rstd)
        qV.op(lambda e: e.tensor_tensor(
            out=lnrow[:, 3 * TD:4 * TD], in0=lnrow[:, 0:TD],
            in1=lnrow[:, TD:2 * TD], op=ALU.mult), cV, 1)
        v_mr = cV.n
        qE.wait(semV, v_mr)
        rP2.acq(qE)
        qE.op(lambda e: e.matmul(
            P2[:, 0:TD], ones_f[:], lnrow[:, TD:2 * TD], start=True, stop=True),
            cT, 1)
        rP3.acq(qE)
        qE.op(lambda e: e.matmul(
            P3[:, 0:TD], ones_f[:], lnrow[:, 3 * TD:4 * TD], start=True, stop=True),
            cT, 1)
        t_bc = cT.n
        r_lnrow.rel(cT)
        qV.wait(semT, t_bc)
        for c in range(KC):
            if out_mode == "x32":
                qV.op(lambda e, c=c, col=col: e.tensor_tensor(
                    out=x32[:, T * c + col:T * c + col + TD],
                    in0=x32[:, T * c + col:T * c + col + TD],
                    in1=P2[:, 0:TD], op=ALU.mult))
                qV.op(lambda e, c=c, col=col: e.tensor_tensor(
                    out=x32[:, T * c + col:T * c + col + TD],
                    in0=x32[:, T * c + col:T * c + col + TD],
                    in1=P3[:, 0:TD], op=ALU.subtract), cV, 1)
            else:
                b = c % 2
                r_obuf[b].acq(qV)
                qV.op(lambda e, c=c, col=col, b=b: e.tensor_tensor(
                    out=o_buf[:, TD * b:TD * b + TD],
                    in0=x32[:, T * c + col:T * c + col + TD],
                    in1=P2[:, 0:TD], op=ALU.mult))
                qV.op(lambda e, b=b: e.tensor_tensor(
                    out=o_buf[:, TD * b:TD * b + TD],
                    in0=o_buf[:, TD * b:TD * b + TD],
                    in1=P3[:, 0:TD], op=ALU.subtract), cV, 1)
                qS.wait(semV, cV.n)
                qS.dma(out_h[128 * c:128 * c + 128, col:col + TD],
                       o_buf[:, TD * b:TD * b + TD], cO2[b])
                r_obuf[b].rel(cO2[b])
        rP2.rel(cV)
        rP3.rel(cV)
        return cV.n

    psum_ctx = ExitStack()
    FP = [psum_ctx.enter_context(nc.psum_tensor(f"fp{i}", [128, 512], F32))
          for i in range(8)]
    rFP = [Res() for _ in range(8)]
    r_wf = [Res(), Res()]
    wf_ready = {}

    def load_wf(ti, k):
        b = k % 2
        r_wf[b].acq(qG)
        qG.dma(wf_buf[:, 1024 * b:1024 * b + 1024],
               wft_h[128 * k:128 * k + 128, :], cW[b])
        wf_ready[(ti, k)] = (cW[b].sem, cW[b].n)

    qE.wait(semV, attn_done_v)
    qE.wait(semA, cA.n)
    qV.wait(semS, cS.n)

    qG.wait(semT, t_vtr)
    load_wf(0, 0)
    load_wf(0, 1)
    ln1_done = {}
    for ti in range(NT):
        for k in range(KC):
            b = k % 2
            qE.wait(*wf_ready[(ti, k)])
            for m in range(KC):
                if k == 0:
                    rFP[m].acq(qE)
                qE.op(lambda e, k=k, m=m, b=b, ti=ti: e.matmul(
                    FP[m][:, 0:TD],
                    wf_buf[:, 1024 * b + 128 * m:1024 * b + 128 * m + 128],
                    attn[:, T * k + TD * ti:T * k + TD * ti + TD],
                    start=(k == 0), stop=(k == KC - 1)),
                    cT if m == KC - 1 else None, 1)
            r_wf[b].rel(cT)
            if k + 2 < KC:
                load_wf(ti, k + 2)
            elif ti + 1 < NT:
                load_wf(ti + 1, k + 2 - KC)
        t_wf = cT.n
        qV.wait(semT, t_wf)
        for m in range(KC):
            qV.op(lambda e, m=m, ti=ti: e.scalar_tensor_tensor(
                out=x32[:, T * m + TD * ti:T * m + TD * ti + TD],
                in0=FP[m][:, 0:TD], scalar=bf_sb[:, m:m + 1],
                in1=x32[:, T * m + TD * ti:T * m + TD * ti + TD],
                op0=ALU.add, op1=ALU.add), cV, 1)
            rFP[m].rel(cV)
        if stop == "p4r":
            ln1_done[ti] = cV.n
        else:
            ln1_done[ti] = emit_ln(ti, "x32", FP[0], FP[1], FP[2], FP[3],
                                   rFP[0], rFP[1], rFP[2], rFP[3])
    psum_ctx.close()

    if stop in ("p4", "p4r"):
        qG.wait(semV, cV.n)
        _finish_dbg([(x32[:, 0:8192], 8192)])
        return _emit_and_close(nc, ctx, qE, qV, qA, qG, qS)

    psum_ctx = ExitStack()
    CP = [psum_ctx.enter_context(nc.psum_tensor(f"cp{i}", [128, 512], F32))
          for i in range(8)]
    rCP = [Res() for _ in range(8)]
    r_wc = [Res(), Res()]
    r_x2b = Res()
    r_yq = [Res() for _ in range(8)]
    r_hq = Res()

    qE.wait(semV, cV.n)
    qE.wait(semA, cA.n)

    wc_seq = [(ti, qq, h, half2) for ti in range(NT) for qq in range(4)
              for h in range(4) for half2 in range(2)]
    wc_ready = {}

    def load_wc(i):
        ti, qq, h, half2 = wc_seq[i]
        b = i % 2
        r_wc[b].acq(qG)
        if i == 0 or i == 1:
            qG.wait(cGc.sem, g_conv)
        row0 = 128 * (32 * h + 8 * qq + 4 * half2)
        qG.dma(wc_buf[:, 4096 * b:4096 * b + 4096].rearrange(
                   "p (c m) -> p c m", c=4),
               wctb_h[row0:row0 + 512, :].rearrange("(c p) m -> p c m", p=128),
               cC[b])
        wc_ready[i] = (cC[b].sem, cC[b].n, b)

    load_wc(0)
    load_wc(1)
    wc_i = 0

    gelu_ready = {}
    pending_ln = [None]
    for ti in range(NT):
        col = TD * ti
        qA.wait(semV, ln1_done[ti])
        r_x2b.acq(qA)
        r_x2b.acq(qV)
        for c in range(KC):
            if c % 2 == 0:
                qA.op(lambda e, c=c, col=col: e.activation(
                    out=x2b[:, TD * c:TD * c + TD],
                    in_=x32[:, T * c + col:T * c + col + TD],
                    func=AF.Copy, bias=0.0, scale=1.0), cA, 1)
            else:
                qV.op(lambda e, c=c, col=col: e.tensor_copy(
                    x2b[:, TD * c:TD * c + TD],
                    x32[:, T * c + col:T * c + col + TD]), cV, 1)
        x2b_a, x2b_v = cA.n, cV.n

        for qq in range(4):
            qE.wait(semA, x2b_a)
            qE.wait(semV, x2b_v)
            emitted_ln = False
            y_ready = {}
            for rpp in range(8):
                rp = 8 * qq + rpp
                g4 = rp % 4
                bank = rpp % 2
                rCP[bank].acq(qE)
                qE.op(lambda e, rp=rp, g4=g4, bank=bank: e.matmul(
                    CP[bank][:, 0:TD],
                    mka_sb[32 * g4:32 * g4 + 32,
                           128 * (rp // 4):128 * (rp // 4) + 128],
                    x2b[32 * g4:32 * g4 + 32, TD * (rp // 4):TD * (rp // 4) + TD],
                    start=True, stop=True, tile_position=(32 * g4, 0)), cT, 1)
                qV.wait(semT, cT.n)
                r_yq[rpp].acq(qV)
                qV.op(lambda e, rpp=rpp, bank=bank: e.tensor_copy(
                    y_q[:, TD * rpp:TD * rpp + TD], CP[bank][:, 0:TD]), cV, 1)
                y_ready[rpp] = cV.n
                rCP[bank].rel(cV)
            if qq == 0 and pending_ln[0] is not None:
                emit_ln(pending_ln[0], "store", CP[6], CP[7], CP[6], CP[7],
                        rCP[6], rCP[7], rCP[6], rCP[7])
                pending_ln[0] = None
            first_gelu = True
            for rpp in range(8):
                rp = 8 * qq + rpp
                qE.wait(semV, y_ready[rpp])
                for h in range(4):
                    bank = 2 + h
                    rCP[bank].acq(qE)
                    qE.op(lambda e, rp=rp, rpp=rpp, h=h, bank=bank: e.matmul(
                        CP[bank][:, 0:TD],
                        upb_sb[32 * h:32 * h + 32, 128 * rp:128 * rp + 128],
                        y_q[32 * h:32 * h + 32, TD * rpp:TD * rpp + TD],
                        start=True, stop=True, tile_position=(32 * h, 0)), cT, 1)
                    t_z = cT.n
                    qA.wait(semT, t_z)
                    if first_gelu:
                        r_hq.acq(qA)
                        first_gelu = False
                    lidx = 8 * h + rpp
                    qA.op(lambda e, lidx=lidx, bank=bank: e.activation(
                        out=h_q[:, TD * lidx:TD * lidx + TD],
                        in_=CP[bank][:, 0:TD], func=AF.Gelu), cA, 1)
                    gelu_ready[lidx] = cA.n
                    rCP[bank].rel(cA)
                r_yq[rpp].rel(cT)
            first = True
            for h in range(4):
                for half2 in range(2):
                    gsem, gn, wb = wc_ready[wc_i]
                    qE.wait(gsem, gn)
                    for cc in range(4):
                        rpg = 4 * half2 + cc
                        lidx = 8 * h + rpg
                        qE.wait(semA, gelu_ready[lidx])
                        if first:
                            for m in range(KC):
                                rCP[m].acq(qE)
                        for m in range(KC):
                            qE.op(lambda e, wb=wb, cc=cc, m=m, lidx=lidx,
                                  st=first, sp=(h == 3 and rpg == 7): e.matmul(
                                CP[m][:, 0:TD],
                                wc_buf[:, 4096 * wb + 1024 * cc + 128 * m:
                                       4096 * wb + 1024 * cc + 128 * m + 128],
                                h_q[:, TD * lidx:TD * lidx + TD],
                                start=st, stop=sp),
                                cT if (m == KC - 1 and cc == 3) else None, 1)
                        first = False
                    r_wc[wb].rel(cT)
                    if wc_i + 2 < len(wc_seq):
                        load_wc(wc_i + 2)
                    wc_i += 1
            t_cproj = cT.n
            r_hq.rel(cT)
            qV.wait(semT, t_cproj)
            for m in range(KC):
                if qq == 0:
                    qV.op(lambda e, m=m: e.tensor_copy(
                        acc[:, TD * m:TD * m + TD], CP[m][:, 0:TD]), cV, 1)
                else:
                    qV.op(lambda e, m=m: e.tensor_tensor(
                        out=acc[:, TD * m:TD * m + TD],
                        in0=acc[:, TD * m:TD * m + TD],
                        in1=CP[m][:, 0:TD], op=ALU.add), cV, 1)
                rCP[m].rel(cV)
        for c in range(KC):
            qV.op(lambda e, c=c, col=col: e.tensor_tensor(
                out=x32[:, T * c + col:T * c + col + TD],
                in0=x32[:, T * c + col:T * c + col + TD],
                in1=acc[:, TD * c:TD * c + TD], op=ALU.add), cV, 1)
        r_x2b.fa = (semT, cT.n)
        pending_ln[0] = ti
    emit_ln(pending_ln[0], "store", CP[6], CP[7], CP[6], CP[7],
            rCP[6], rCP[7], rCP[6], rCP[7])

    qS.wait(cO2[0].sem, cO2[0].n)
    qS.wait(cO2[1].sem, cO2[1].n)
    return _emit_and_close(nc, ctx, qE, qV, qA, qG, qS)


def _emit_and_close(nc, ctx, qE, qV, qA, qG, qS):
    with nc.Block() as block:
        @block.tensor
        def _(e):
            qE.emit(e)

        @block.vector
        def _(e):
            qV.emit(e)

        @block.scalar
        def _(e):
            qA.emit(e)

        @block.gpsimd
        def _(e):
            qG.emit(e)

        @block.sync
        def _(e):
            qS.emit(e)

    ctx.close()
    return nc



def _host_prep(inputs):
    _ = inputs.get("L0_attn_segment_pos")
    x = np.asarray(inputs["L0_attn_x"], dtype=np.float32)
    Wq = np.asarray(inputs["L0_attn_Wq"], dtype=np.float32)
    Wk = np.asarray(inputs["L0_attn_Wk"], dtype=np.float32)
    Wv = np.asarray(inputs["L0_attn_Wv"], dtype=np.float32)
    Wf = np.asarray(inputs["L0_attn_Wf"], dtype=np.float32)
    bfv = np.asarray(inputs["L0_attn_bf"], dtype=np.float32)
    mk = np.asarray(inputs["L0_ffn_rail_maker"], dtype=np.float32)
    up = np.asarray(inputs["L0_ffn_rail_up_proj"], dtype=np.float32)
    Wc = np.asarray(inputs["L0_ffn_w_cproj"], dtype=np.float32)

    shared = {}
    shared["wqt"] = np.ascontiguousarray(Wq.T)
    shared["wkd"] = np.ascontiguousarray(np.concatenate([Wk.T, Wk.T], axis=1))
    shared["wvt"] = np.ascontiguousarray(Wv.T)
    shared["wft"] = np.ascontiguousarray(Wf.T)
    shared["wct"] = np.ascontiguousarray(Wc.T)
    shared["bfr"] = np.ascontiguousarray(bfv.reshape(KC, 128).T)
    shared["ident"] = np.eye(128, dtype=np.float32)

    mka = np.zeros((128, 1024), dtype=np.float32)
    h_, r_, i_, j_ = np.indices((4, 64, 16, 16))
    rp, rr = r_ // 2, r_ % 2
    p = 32 * (rp % 4) + 16 * rr + i_
    cc = 128 * (rp // 4) + 32 * h_ + 16 * rr + j_
    mka[p.ravel(), cc.ravel()] = mk.ravel()
    shared["mka"] = mka

    upb = np.zeros((128, 4096), dtype=np.float32)
    h_, r_, k_, j_ = np.indices((4, 64, 64, 16))
    rp, rr = r_ // 2, r_ % 2
    p = 32 * h_ + 16 * rr + j_
    cc = 128 * rp + 64 * rr + k_
    upb[p.ravel(), cc.ravel()] = up.ravel()
    shared["upb"] = upb

    i_g, j_g = np.indices((128, WIN))
    band = (j_g >= i_g) & (j_g <= i_g + 256)
    neg = np.float32(-1e30)
    m0 = np.where(band, np.float32(0.0), neg).astype(np.float32)
    m1s = np.where(band & (j_g >= 256), np.float32(0.0), neg)
    m2s = np.where(band & (j_g >= 128), np.float32(0.0), neg)
    masks_plain = np.ascontiguousarray(np.concatenate([m0, m0, m0], axis=1))
    masks_start = np.ascontiguousarray(
        np.concatenate([m0, m1s, m2s], axis=1).astype(np.float32))

    in_maps = []
    for core in range(8):
        b = core // 4
        r0 = (core % 4) * T
        xs = np.zeros((TH, D), dtype=np.float32)
        lo = max(0, r0 - HALO)
        xs[HALO - (r0 - lo):] = x[b, lo:r0 + T]
        d = dict(shared)
        d["x"] = np.ascontiguousarray(xs.T)
        d["masks"] = masks_start if r0 == 0 else masks_plain
        in_maps.append(d)
    return in_maps


def kernel(**inputs):
    in_maps = _host_prep(inputs)
    nc = build_program()
    res = run_bass_kernel_spmd(nc, in_maps, core_ids=list(range(8)))
    out = np.empty((2, 8192, D), dtype=np.float32)
    for core in range(8):
        b = core // 4
        r0 = (core % 4) * T
        out[b, r0:r0 + T, :] = res.results[core]["out"].T
    return out


if __name__ == "__main__":
    nc = build_program()
    print("built ok")
